# version 16
# baseline (speedup 1.0000x reference)
import contextlib
import numpy as np

import concourse.bass as bass
import concourse.mybir as mybir
from concourse.bass_utils import run_bass_kernel_spmd

F32 = mybir.dt.float32
BF16 = mybir.dt.bfloat16
I32 = mybir.dt.int32
AF = mybir.ActivationFunctionType
ALU = mybir.AluOpType
AX = mybir.AxisListType.X

P = 128
DM = 2048
S = 4096
TOWN = 2048
NHEADS = 16
DE = 8192
QMUL = float(np.float32(1.0 / np.sqrt(128.0)))
LN_EPS = 1e-5
FQ_EPS = 1e-8
TERN_EPS = 1e-8
FBIG = 3.0e38

N_CORES = 8
REPL = [list(range(N_CORES))]


class Sem:
    def __init__(self, stack, nc, name):
        self.h = stack.enter_context(nc.semaphore(name))
        self.v = 0

    def inc(self, instr, n=1):
        instr.then_inc(self.h, n)
        self.v += n
        return instr

    def dinc(self, instr):
        return self.inc(instr, 16)

    def ck(self):
        return (self, self.v)

    def wait(self, eng, v=None):
        eng.wait_ge(self.h, self.v if v is None else v)


def wck(eng, ck):
    if ck is not None:
        sem, v = ck
        eng.wait_ge(sem.h, v)


class DmaSlots:

    def __init__(self, stack, nc, name, n=2):
        self.sems = [Sem(stack, nc, f"{name}{d}") for d in range(n)]
        self.n = n

    def issue(self, eng, i, out, in_, transpose=False):
        s = self.sems[i % self.n]
        if transpose:
            s.dinc(eng.dma_start_transpose(out, in_))
        else:
            s.dinc(eng.dma_start(out=out, in_=in_))
        return s.ck()

    def wait_all(self, eng):
        for s in self.sems:
            s.wait(eng)


def build_program(dbg=()):
    nc = bass.Bass("TRN2", target_bir_lowering=False)
    st = contextlib.ExitStack()

    x_tok = nc.dram_tensor("x_tok", [S, DM], F32, kind="ExternalInput").ap()
    x_fmh = nc.dram_tensor("x_fmh", [DM, TOWN], F32, kind="ExternalInput").ap()
    w_t = {k: nc.dram_tensor(f"w{k}_t", [DM, DM], F32, kind="ExternalInput").ap()
           for k in "qkvp"}
    b2d = {k: nc.dram_tensor(f"b{k}2d", [P, 16], F32, kind="ExternalInput").ap()
           for k in "qkvp"}
    wu_t = nc.dram_tensor("wu_t", [DM, DE], F32, kind="ExternalInput").ap()
    wg_t = nc.dram_tensor("wg_t", [DM, DE], F32, kind="ExternalInput").ap()
    wd_t = nc.dram_tensor("wd_t", [DE, DM], F32, kind="ExternalInput").ap()
    eye_in = nc.dram_tensor("eye128", [P, P], F32, kind="ExternalInput").ap()
    out_fm = nc.dram_tensor("out_fm", [DM, TOWN], F32, kind="ExternalOutput").ap()

    dscr = lambda name, shape, dt: nc.dram_tensor(name, shape, dt).ap()
    nh_sp = dscr("nh_sp", [S, DM], BF16)
    q_sp = dscr("q_sp", [DM, TOWN], F32)
    k_sp = dscr("k_sp", [DM, S], F32)
    v_sp = dscr("v_sp", [DM, S], F32)
    nq_sp = dscr("nq_sp", [DM, TOWN], BF16)
    nk_sp = dscr("nk_sp", [DM, S], BF16)
    nv_sp = dscr("nv_sp", [DM, S], BF16)
    nvtm_bk = dscr("nvtm_bk", [S * (DM // P), P], BF16)
    nw_bk = {k: dscr(f"nw{k}_bk", [DM * (DM // P), P], BF16) for k in "qkvp"}
    tu_bk = dscr("tu_bk", [DM * (DE // P), P], BF16)
    tg_bk = dscr("tg_bk", [DM * (DE // P), P], BF16)
    td_bk = dscr("td_bk", [DE * (DM // P), P], BF16)
    out_sp = dscr("out_sp", [DM, TOWN], F32)
    x1_sp = dscr("x1_sp", [DM, TOWN], F32)
    bt_dram = dscr("bt_dram", [256, P], F32)
    cc_in = {i: dscr(f"cc{i}_in", [1, 4], F32) for i in range(1, 5)}
    cc_out = {i: dscr(f"cc{i}_out", [1, 4], F32) for i in range(1, 5)}

    def blk_w(blk, rc, ncols):
        return blk[rc * ncols:(rc + 1) * ncols, :]

    def blk_r(blk, cc, ncols, nrows):
        return blk.rearrange("(r c p) m -> c p r m", c=ncols // P, p=P)[cc]

    dbg_t = {}
    _dbgnames = []
    for name in dbg:
        _dbgnames += (["q", "k", "v"] if name == "qkv" else
                      ["mstat", "zstat", "brow"] if name == "mz" else [name])
    for name in _dbgnames:
        shape, dt = DBG_SHAPES[name]
        dbg_t[name] = nc.dram_tensor("dbg_" + name, shape, dt,
                                     kind="ExternalOutput").ap()

    sb = lambda name, shape, dt: st.enter_context(nc.sbuf_tensor(name, shape, dt))
    ones_row = sb("ones_row", [1, P], F32)
    ones_col = sb("ones_col", [P, 1], F32)
    eye = sb("eye", [P, P], F32)
    bias_sb = {k: sb(f"bias_{k}", [P, 16], F32) for k in "qkvp"}
    ln_m = sb("ln_m", [P, 32], F32)
    ln_r = sb("ln_r", [P, 32], F32)
    ln_a = sb("ln_a", [P, 32], F32)
    ln_b = sb("ln_b", [P, 32], F32)
    mstat = sb("mstat", [P, 256], F32)
    zstat = sb("zstat", [P, 256], F32)
    BCN = 32
    bc = sb("bc", [P, BCN], F32)
    BC = {}
    sc = sb("sc", [1, 96], F32)
    SC = {}
    SC_STAGE = 88
    amax = sb("amax", [P, 12], F32)
    AM = {}
    zmin = sb("zmin", [P, 1], F32)
    srow = sb("srow", [1, 2048], F32)
    mrrow = sb("mrrow", [1, 2048], F32)

    def bc_col(name):
        if name not in BC:
            assert len(BC) < BCN, "bc overflow"
            BC[name] = bc[:, len(BC):len(BC) + 1]
        return BC[name]

    def sc_col(name):
        if name not in SC:
            assert len(SC) < SC_STAGE, "sc overflow"
            SC[name] = sc[:, len(SC):len(SC) + 1]
        return SC[name]

    def am_col(name):
        if name not in AM:
            assert len(AM) < 12, "amax overflow"
            AM[name] = amax[:, len(AM):len(AM) + 1]
        return AM[name]

    te, ve, ac, gp, sy = nc.tensor, nc.vector, nc.scalar, nc.gpsimd, nc.sync
    LD = DmaSlots(st, nc, "LD", 2)
    LD2 = DmaSlots(st, nc, "LE", 2)
    SP = DmaSlots(st, nc, "SP", 4)
    CS = DmaSlots(st, nc, "CS", 4)
    F1 = DmaSlots(st, nc, "F1", 2)
    F2 = DmaSlots(st, nc, "F2", 2)
    sTE = Sem(st, nc, "sTE")
    sDV = Sem(st, nc, "sDV")
    sAC = Sem(st, nc, "sAC")
    sGP = Sem(st, nc, "sGP")
    sCC = Sem(st, nc, "sCC")
    sMS = Sem(st, nc, "sMS")

    def barrier():
        nc.all_engine_barrier()

    def vd():
        ve.drain()

    _mps = {"stack": None, "t": None, "n": 0}

    def misc_open():
        _mps["stack"] = contextlib.ExitStack()
        _mps["n"] += 1
        _mps["t"] = _mps["stack"].enter_context(
            nc.psum_tensor(f"misc_ps{_mps['n']}", [P, 512], F32))

    def misc_close():
        _mps["stack"].close()

    misc_open()

    class _MiscPs:
        def __getitem__(self, sl):
            return _mps["t"][sl]

    misc_ps = _MiscPs()

    def bcast(dst_col, src11):
        sDV.wait(te)
        sTE.inc(te.matmul(misc_ps[:, 0:1], ones_row[:, :], src11,
                          start=True, stop=True))
        sTE.wait(ve)
        sDV.inc(ve.tensor_copy(dst_col, misc_ps[:, 0:1]))
        sDV.wait(te)
        sDV.wait(ac)
        sDV.wait(gp)

    def preduce(dst11, src_col, op):
        sDV.wait(te)
        sTE.inc(te.transpose(misc_ps[0:1, 0:P], src_col, eye[:, :]))
        sTE.wait(ve)
        sDV.inc(ve.tensor_reduce(dst11, misc_ps[0:1, 0:P], AX, op))
        sDV.wait(te)
        vd()

    def fq_scale(name_amax, name_s, name_inv):
        vd()
        ve.tensor_scalar(sc_col(name_s), sc_col(name_amax), 1.0 / 127.0,
                         FQ_EPS, ALU.mult, ALU.max)
        vd()
        sDV.inc(ve.reciprocal(sc_col(name_inv), sc_col(name_s)))
        vd()

    def collective(idx, names_in, names_out, op):
        n = len(names_in)
        sDV.wait(ve)
        vd()
        for j, nm in enumerate(names_in):
            ve.tensor_copy(sc[0:1, SC_STAGE + j:SC_STAGE + j + 1], sc_col(nm))
        sDV.inc(ve.engine_nop())
        sDV.wait(gp)
        sMS.dinc(gp.dma_start(out=cc_in[idx][0:1, 0:n],
                              in_=sc[0:1, SC_STAGE:SC_STAGE + n]))
        sMS.wait(gp)
        sCC.inc(gp.collective_compute(
            "AllReduce", op, replica_groups=REPL,
            ins=[cc_in[idx][0:1, 0:n]], outs=[cc_out[idx][0:1, 0:n]]))
        sCC.wait(gp)
        sMS.dinc(gp.dma_start(out=sc[0:1, SC_STAGE:SC_STAGE + n],
                              in_=cc_out[idx][0:1, 0:n]))
        sMS.wait(gp)
        sGP.inc(gp.engine_nop())
        sGP.wait(ve)
        for j, nm in enumerate(names_out):
            ve.tensor_copy(sc_col(nm), sc[0:1, SC_STAGE + j:SC_STAGE + j + 1])
        sDV.inc(ve.engine_nop())
        vd()

    def dbg_dump_dram(name, src_ap, rows, cols, dt):
        with nc.sbuf_tensor("dbgb_" + name, [P, cols], dt) as t:
            for i in range(rows // P):
                sMS.dinc(gp.dma_start(out=t[:, :],
                                      in_=src_ap[i * P:(i + 1) * P, :]))
                sMS.wait(gp)
                sMS.dinc(gp.dma_start(out=dbg_t[name][i * P:(i + 1) * P, :],
                                      in_=t[:, :]))
                sMS.wait(gp)

    ve.memset(ones_row[:, :], 1.0)
    ve.memset(ones_col[:, :], 1.0)
    ve.memset(zmin[:, :], FBIG)
    ve.memset(amax[:, :], 0.0)
    sMS.dinc(sy.dma_start(out=eye[:, :], in_=eye_in[:, :]))
    for k in "qkvp":
        sMS.dinc(sy.dma_start(out=bias_sb[k][:, :], in_=b2d[k][:, :]))
    sMS.wait(sy)
    barrier()

    NT = S // P
    with nc.sbuf_tensor("xa", [P, 2 * DM], F32) as xa, \
         nc.sbuf_tensor("sq_a", [P, DM], F32) as sq_a, \
         nc.sbuf_tensor("sq_b", [P, DM], F32) as sq_b, \
         nc.sbuf_tensor("p1tmp", [P, 12], F32) as tp:
        s1 = tp[:, 0:1]; mm_ = tp[:, 2:3]; m2 = tp[:, 3:4]
        vv = tp[:, 4:5]; sd = tp[:, 5:6]; rr = tp[:, 6:7]; hx = tp[:, 7:8]
        dv_done = [None] * NT
        ac_done = [None] * NT
        for i in range(NT):
            buf = xa[:, (i % 2) * DM:(i % 2) * DM + DM]
            s2 = tp[:, 8 + i % 2:9 + i % 2]
            if i >= 2:
                wck(sy, dv_done[i - 2])
                wck(sy, ac_done[i - 2])
            ld = LD.issue(sy, i, buf, x_tok[i * P:(i + 1) * P, :])
            wck(ve, ld)
            ve.reduce_sum(s1, buf, AX)
            wck(ac, ld)
            sAC.inc(ac.activation(sq_a[:, :], buf, AF.Square, accum_out=s2))
            ac_done[i] = sAC.ck()
            vd()
            ve.tensor_scalar(mm_, s1, 1.0 / DM, None, ALU.mult)
            sAC.wait(ve)
            vd()
            ve.tensor_mul(m2, mm_, mm_)
            ve.tensor_scalar(vv, s2, 1.0 / DM, LN_EPS, ALU.mult, ALU.add)
            vd()
            ve.tensor_sub(vv, vv, m2)
            ve.tensor_copy(ln_m[:, i:i + 1], mm_)
            sDV.inc(ve.engine_nop())
            sDV.wait(ac)
            sAC.inc(ac.activation(sd, vv, AF.Sqrt))
            sAC.wait(ve)
            ve.reciprocal(rr, sd)
            vd()
            ve.tensor_copy(ln_r[:, i:i + 1], rr)
            ve.tensor_scalar(sq_b[:, :], buf, mm_, None, ALU.subtract)
            ve.tensor_reduce(hx, sq_b[:, :], AX, ALU.max,
                             apply_absolute_value=True)
            vd()
            ve.tensor_mul(hx, hx, rr)
            vd()
            sDV.inc(ve.tensor_tensor(am_col("h"), am_col("h"), hx, ALU.max))
            dv_done[i] = sDV.ck()
    preduce(sc_col("amax_h"), am_col("h"), ALU.max)
    collective(1, ["amax_h"], ["amax_h_g"], ALU.max)
    sDV.wait(ve)
    fq_scale("amax_h_g", "s_h", "inv_sh")
    bcast(bc_col("inv_sh"), sc_col("inv_sh"))
    sDV.wait(ve)
    ve.tensor_scalar(ln_a[:, :], ln_r[:, :], bc_col("inv_sh"), None, ALU.mult)
    vd()
    ve.tensor_mul(ln_b[:, :], ln_m[:, :], ln_a[:, :])
    vd()
    sDV.inc(ve.tensor_scalar(ln_b[:, :], ln_b[:, :], -1.0, None, ALU.mult))
    vd()
    barrier()

    def quant_stream(src, ntile, fd, scale_cols, bias_cols=None, clip=False,
                     xbuf=None, ibuf=None, obuf2=None, dst=None, dst_blk=None,
                     nbuf=2):
        ac_done = [None] * ntile
        cv_done = [None] * ntile
        sp_done = [None] * ntile
        for i in range(ntile):
            buf = xbuf[:, (i % nbuf) * fd:(i % nbuf) * fd + fd]
            if i >= nbuf:
                wck(sy, ac_done[i - nbuf])
            ld = LD.issue(sy, i, buf, src[i * P:(i + 1) * P, :])
            wck(ac, ld)
            if i >= nbuf:
                wck(ac, cv_done[i - nbuf])
            kw = {}
            if scale_cols is not None:
                kw["scale"] = scale_cols[i] if isinstance(scale_cols, list) \
                    else scale_cols
            if bias_cols is not None:
                kw["bias"] = bias_cols[i]
            ib = ibuf[:, (i % nbuf) * fd:(i % nbuf) * fd + fd]
            sAC.inc(ac.activation(ib, buf, AF.Identity, **kw))
            ac_done[i] = sAC.ck()
            if clip:
                ob = obuf2[:, (i % 2) * fd:(i % 2) * fd + fd]
                assert nbuf == 2
                if i >= 2:
                    wck(ve, sp_done[i - 2])
                sAC.wait(ve)
                sDV.inc(ve.tensor_scalar(ob, ib, -1.0, 1.0,
                                         ALU.max, ALU.min))
                cv_done[i] = sDV.ck()
                sDV.wait(gp)
                if dst_blk is not None:
                    sp_done[i] = SP.issue(
                        gp, i,
                        blk_w(dst_blk, i, fd).rearrange("(c p) m -> p c m",
                                                        p=P),
                        ob.rearrange("p (c m) -> p c m", m=P))
                else:
                    sp_done[i] = SP.issue(gp, i, dst[i * P:(i + 1) * P, :], ob)
            else:
                sAC.wait(gp)
                if dst_blk is not None:
                    ck2 = SP.issue(
                        gp, i,
                        blk_w(dst_blk, i, fd).rearrange("(c p) m -> p c m",
                                                        p=P),
                        ib.rearrange("p (c m) -> p c m", m=P))
                else:
                    ck2 = SP.issue(gp, i, dst[i * P:(i + 1) * P, :], ib)
                cv_done[i] = ck2
                sp_done[i] = ck2
        SP.wait_all(gp)

    with nc.sbuf_tensor("xb", [P, 4 * DM], F32) as xb, \
         nc.sbuf_tensor("nhi", [P, 4 * DM], I32) as nhi:
        quant_stream(x_tok, NT, DM,
                     [ln_a[:, i:i + 1] for i in range(NT)],
                     bias_cols=[ln_b[:, i:i + 1] for i in range(NT)],
                     xbuf=xb, ibuf=nhi, dst=nh_sp, nbuf=4)
    barrier()
    if "nh" in dbg:
        dbg_dump_dram("nh", nh_sp, S, DM, BF16)
        barrier()

    with nc.sbuf_tensor("ft", [P, 3 * DE], F32) as ft, \
         nc.sbuf_tensor("fred", [P, 1], F32) as fred, \
         nc.sbuf_tensor("fsum", [P, 1], F32) as fsum:
        for (w_in, w_bk, nrows, name) in ((wu_t, tu_bk, DM, "u"),
                                          (wg_t, tg_bk, DM, "g"),
                                          (wd_t, td_bk, DE, "d")):
            ntile = nrows // P
            fd = w_in.shape[1]
            ve.memset(fsum[:, :], 0.0)
            dv_done = [None] * ntile
            for i in range(ntile):
                buf = ft[:, (i % 3) * DE:(i % 3) * DE + fd]
                if i >= 3:
                    wck(sy, dv_done[i - 3])
                ld = LD.issue(sy, i, buf, w_in[i * P:(i + 1) * P, :])
                wck(ve, ld)
                ve.tensor_reduce(fred[:, :], buf, AX, ALU.add,
                                 apply_absolute_value=True)
                vd()
                sDV.inc(ve.tensor_add(fsum[:, :], fsum[:, :], fred[:, :]))
                dv_done[i] = sDV.ck()
            preduce(sc_col("tsum" + name), fsum[:, :], ALU.add)
            ve.tensor_scalar(sc_col("tscale" + name), sc_col("tsum" + name),
                             1.0 / (DM * DE), TERN_EPS, ALU.mult, ALU.add)
            vd()
            sDV.inc(ve.reciprocal(sc_col("tinv" + name),
                                  sc_col("tscale" + name)))
            vd()
            bcast(bc_col("tinv" + name), sc_col("tinv" + name))
            barrier()
    barrier()


    st_nh = contextlib.ExitStack()
    nh_fm = st_nh.enter_context(nc.sbuf_tensor("nh_fm", [P, 16 * S], BF16))
    for j in range(16):
        LD.issue(sy, j, nh_fm[:, j * S:(j + 1) * S],
                 nh_sp[:, j * P:(j + 1) * P], transpose=True)
    LD.wait_all(sy)
    barrier()

    with nc.sbuf_tensor("wt", [P, 2 * DM], F32) as wt, \
         nc.sbuf_tensor("wi", [P, 2 * DM], I32) as wi, \
         nc.sbuf_tensor("wred", [P, 1], F32) as wred:
        for k in "qkvp":
            dv_done = [None] * 16
            for i in range(16):
                buf = wt[:, (i % 2) * DM:(i % 2) * DM + DM]
                if i >= 2:
                    wck(sy, dv_done[i - 2])
                ld = LD.issue(sy, i, buf, w_t[k][i * P:(i + 1) * P, :])
                wck(ve, ld)
                ve.tensor_reduce(wred[:, :], buf, AX, ALU.max,
                                 apply_absolute_value=True)
                vd()
                sDV.inc(ve.tensor_tensor(am_col("w" + k), am_col("w" + k),
                                         wred[:, :], ALU.max))
                dv_done[i] = sDV.ck()
            preduce(sc_col("amax_w" + k), am_col("w" + k), ALU.max)
            fq_scale("amax_w" + k, "s_w" + k, "inv_w" + k)
            bcast(bc_col("inv_w" + k), sc_col("inv_w" + k))
            barrier()
            quant_stream(w_t[k], 16, DM, bc_col("inv_w" + k),
                         xbuf=wt, ibuf=wi, dst_blk=nw_bk[k])
            barrier()
    barrier()

    vd()
    for k in "qkv":
        ve.tensor_mul(sc_col("dq_" + k), sc_col("s_h"), sc_col("s_w" + k))
    vd()
    ve.tensor_scalar(sc_col("dq_q"), sc_col("dq_q"), QMUL, None, ALU.mult)
    ve.tensor_scalar(bias_sb["q"][:, :], bias_sb["q"][:, :], QMUL, None,
                     ALU.mult)
    vd()
    sDV.inc(ve.engine_nop())
    for k in "qkv":
        bcast(bc_col("dq_" + k), sc_col("dq_" + k))
    barrier()

    with nc.sbuf_tensor("nwb", [P, 2 * DM], BF16) as nwb, \
         nc.sbuf_tensor("pstg", [P, 4 * 512], F32) as pstg, \
         nc.sbuf_tensor("pred", [P, 1], F32) as pred, \
         nc.psum_tensor("proj_ps", [P, 2 * 512], F32) as proj_ps:
        for k, tok, spill in (("q", TOWN, q_sp), ("k", S, k_sp), ("v", S, v_sp)):
            if k != "q":
                SP.wait_all(gp)
                barrier()
            ntc = tok // 512
            it = 0
            w_used = [None] * 16
            ac_done = {}
            dv_done = {}
            sp_done = {}
            for oc in range(16):
                if oc >= 2:
                    wck(sy, w_used[oc - 2])
                wbuf = nwb[:, (oc % 2) * DM:(oc % 2) * DM + DM]
                ld = LD.issue(
                    sy, oc,
                    wbuf.rearrange("p (j m) -> p j m", m=P),
                    blk_r(nw_bk[k], oc, DM, DM))
                for tc in range(ntc):
                    ps = proj_ps[:, (it % 2) * 512:(it % 2) * 512 + 512]
                    wck(te, ld)
                    if it >= 2:
                        wck(te, ac_done.get(it - 2))
                    for j in range(16):
                        ins = te.matmul(
                            ps, wbuf[:, j * P:(j + 1) * P],
                            nh_fm[:, j * S + tc * 512:j * S + tc * 512 + 512],
                            start=(j == 0), stop=(j == 15))
                    sTE.inc(ins)
                    stg = pstg[:, (it % 4) * 512:(it % 4) * 512 + 512]
                    sTE.wait(ac)
                    if it >= 4:
                        wck(ac, sp_done.get(it - 4))
                        wck(ac, dv_done.get(it - 4))
                    sAC.inc(ac.activation(stg, ps, AF.Identity,
                                          bias=bias_sb[k][:, oc:oc + 1],
                                          scale=bc_col("dq_" + k)))
                    ac_done[it] = sAC.ck()
                    sAC.wait(ve)
                    ve.tensor_reduce(pred[:, :], stg, AX, ALU.max,
                                     apply_absolute_value=True)
                    vd()
                    sDV.inc(ve.tensor_tensor(am_col(k), am_col(k),
                                             pred[:, :], ALU.max))
                    dv_done[it] = sDV.ck()
                    sAC.wait(gp)
                    sp_done[it] = SP.issue(
                        gp, it,
                        spill[oc * P:(oc + 1) * P, tc * 512:tc * 512 + 512],
                        stg)
                    it += 1
                w_used[oc] = sTE.ck()
        SP.wait_all(gp)
    st_nh.close()
    barrier()
    if "qkv" in dbg:
        dbg_dump_dram("q", q_sp, DM, TOWN, F32)
        dbg_dump_dram("k", k_sp, DM, S, F32)
        dbg_dump_dram("v", v_sp, DM, S, F32)
        barrier()

    for k in "qkv":
        preduce(sc_col("amax_" + k), am_col(k), ALU.max)
    collective(2, ["amax_q", "amax_k", "amax_v"], ["gq", "gk", "gv"], ALU.max)
    sDV.wait(ve)
    for k in "qkv":
        fq_scale("g" + k, "s_" + k, "inv_" + k)
        bcast(bc_col("inv_" + k), sc_col("inv_" + k))
    vd()
    ve.tensor_mul(sc_col("sqsk"), sc_col("s_q"), sc_col("s_k"))
    vd()
    sDV.inc(ve.engine_nop())
    bcast(bc_col("sqsk"), sc_col("sqsk"))
    barrier()

    with nc.sbuf_tensor("qt4", [P, 2 * S], F32) as qt4, \
         nc.sbuf_tensor("qi4", [P, 2 * S], I32) as qi4, \
         nc.sbuf_tensor("qb4", [P, 2 * S], BF16) as qb4:
        for k, tok, spill, qspill in (("q", TOWN, q_sp, nq_sp),
                                      ("k", S, k_sp, nk_sp),
                                      ("v", S, v_sp, nv_sp)):
            quant_stream(spill, 16, tok, bc_col("inv_" + k),
                         xbuf=qt4, ibuf=qi4, dst=qspill)
            barrier()
        barrier()
        for c in range(32):
            buf = qb4[:, (c % 2) * S:(c % 2) * S + DM]
            ld = LD.issue(sy, c, buf, nv_sp[:, c * P:(c + 1) * P],
                          transpose=True)
            wck(gp, ld)
            spk = SP.issue(gp, c,
                           blk_w(nvtm_bk, c, DM).rearrange("(c2 p) m -> p c2 m",
                                                           p=P),
                           buf.rearrange("p (c2 m) -> p c2 m", m=P))
            wck(sy, spk)
    barrier()

    NQT = TOWN // P
    misc_close()

    def make_ffnq_pump(fqx, fqi, fqb):
        jobs = []
        for (w_in, w_bk, nrows, name) in ((wu_t, tu_bk, DM, "u"),
                                          (wg_t, tg_bk, DM, "g"),
                                          (wd_t, td_bk, DE, "d")):
            fd = w_in.shape[1]
            nh_ = 2 if fd == DE else 1
            cfd = fd // nh_
            for i in range(nrows // P):
                for hf in range(nh_):
                    jobs.append((w_in, w_bk, i, hf, fd, cfd, name))

        def gen():
            ac_d = {}
            cv_d = {}
            sp_d = {}
            for j, (w_in, w_bk, i, hf, fd, cfd, name) in enumerate(jobs):
                buf = fqx[:, (j % 2) * 4096:(j % 2) * 4096 + cfd]
                if j >= 2:
                    wck(sy, ac_d[j - 2])
                ld = F1.issue(sy, j, buf,
                              w_in[i * P:(i + 1) * P,
                                   hf * cfd:(hf + 1) * cfd])
                wck(ac, ld)
                if j >= 2:
                    wck(ac, cv_d[j - 2])
                ib = fqi[:, (j % 2) * 4096:(j % 2) * 4096 + cfd]
                sAC.inc(ac.activation(ib, buf, AF.Identity,
                                      scale=bc_col("tinv" + name)))
                ac_d[j] = sAC.ck()
                ob = fqb[:, (j % 2) * 4096:(j % 2) * 4096 + cfd]
                wck(gp, ac_d[j])
                if j >= 2:
                    wck(gp, sp_d[j - 2])
                sGP.inc(gp.tensor_scalar(ob, ib, -1.0, 1.0,
                                         ALU.max, ALU.min))
                cv_d[j] = sGP.ck()
                dst = w_bk[i * fd + hf * cfd:i * fd + (hf + 1) * cfd, :]
                sp_d[j] = F2.issue(
                    gp, j, dst.rearrange("(c p) m -> p c m", p=P),
                    ob.rearrange("p (c m) -> p c m", m=P))
                yield
        return gen()

    with nc.sbuf_tensor("fqx", [P, 2 * 4096], F32) as fqx, \
         nc.sbuf_tensor("fqi", [P, 2 * 4096], I32) as fqi, \
         nc.sbuf_tensor("fqb", [P, 2 * 4096], BF16) as fqb, \
         nc.sbuf_tensor("p5q", [P, 2 * TOWN], BF16) as p5q, \
         nc.sbuf_tensor("p5k", [P, 2 * S], BF16) as p5k, \
         nc.sbuf_tensor("p5e", [P, 2048], F32) as p5e, \
         nc.sbuf_tensor("p5t", [P, 16], F32) as p5t, \
         nc.sbuf_tensor("zh_arr", [P, 32], F32) as zh_arr, \
         nc.sbuf_tensor("ea_arr", [P, 32], F32) as ea_arr, \
         nc.sbuf_tensor("ee_arr", [P, 32], F32) as ee_arr, \
         nc.sbuf_tensor("tt_arr", [P, 32], F32) as tt_arr, \
         nc.sbuf_tensor("zrow", [P, 16], F32) as zrow, \
         nc.sbuf_tensor("zred", [P, 1], F32) as zred, \
         nc.psum_tensor("p1_ps", [P, S], F32) as p1_ps:
        pump = make_ffnq_pump(fqx, fqi, fqb)

        def pump_n(n):
            for _ in range(n):
                try:
                    next(pump)
                except StopIteration:
                    return
        exp_cks = {}
        head_used = [None] * NHEADS
        merge_ck = None
        ee_ck = None
        for h in range(NHEADS):
            qb = p5q[:, (h % 2) * TOWN:(h % 2) * TOWN + TOWN]
            kb = p5k[:, (h % 2) * S:(h % 2) * S + S]
            if h >= 2:
                wck(sy, head_used[h - 2])
            ld1 = LD.issue(sy, h, qb, nq_sp[h * P:(h + 1) * P, :])
            ld2 = LD2.issue(sy, h, kb, nk_sp[h * P:(h + 1) * P, :])
            for i in range(NQT):
                idx = h * NQT + i
                sl = 8 * (idx % 2)
                mh = p5t[:, sl + 0:sl + 2]
                nb = p5t[:, sl + 2:sl + 4]
                mi = p5t[:, sl + 4:sl + 5]
                te_ck = [None, None]
                for half in range(2):
                    wck(te, ld1)
                    wck(te, ld2)
                    if idx > 0:
                        wck(te, exp_cks[(idx - 1, half)])
                    for cchunk in range(4):
                        off = half * 2048 + cchunk * 512
                        ins = te.matmul(p1_ps[:, off:off + 512],
                                        qb[:, i * P:(i + 1) * P],
                                        kb[:, off:off + 512],
                                        start=True, stop=True)
                    sTE.inc(ins)
                    te_ck[half] = sTE.ck()
                wck(ve, te_ck[0])
                if idx > 1:
                    wck(ve, exp_cks[(idx - 2, 1)])
                if i == 0 and ee_ck is not None:
                    wck(ve, ee_ck)
                ve.tensor_reduce(mh[:, 0:1], p1_ps[:, 0:2048], AX, ALU.max)
                vd()
                ve.tensor_scalar(nb[:, 0:1], mh[:, 0:1], bc_col("sqsk"),
                                 -1.0, ALU.mult, ALU.mult)
                vd()
                sDV.inc(ve.engine_nop())
                dv0 = sDV.ck()
                wck(ac, dv0)
                if i == 0 and merge_ck is not None:
                    wck(ac, merge_ck)
                sAC.inc(ac.activation(p5e[:, :], p1_ps[:, 0:2048], AF.Exp,
                                      bias=nb[:, 0:1], scale=bc_col("sqsk"),
                                      accum_out=zh_arr[:, 2 * i:2 * i + 1]))
                exp_cks[(idx, 0)] = sAC.ck()
                wck(ve, te_ck[1])
                ve.tensor_reduce(mh[:, 1:2], p1_ps[:, 2048:4096], AX, ALU.max)
                vd()
                ve.tensor_scalar(nb[:, 1:2], mh[:, 1:2], bc_col("sqsk"),
                                 -1.0, ALU.mult, ALU.mult)
                ve.tensor_tensor(mi, mh[:, 0:1], mh[:, 1:2], ALU.max)
                vd()
                ve.tensor_copy(mstat[:, idx:idx + 1], mi)
                ve.tensor_scalar(ea_arr[:, 2 * i:2 * i + 2], mh, mi,
                                 bc_col("sqsk"), ALU.subtract, ALU.mult)
                vd()
                sDV.inc(ve.engine_nop())
                dv1 = sDV.ck()
                wck(ac, dv1)
                sAC.inc(ac.activation(p5e[:, :], p1_ps[:, 2048:4096], AF.Exp,
                                      bias=nb[:, 1:2], scale=bc_col("sqsk"),
                                      accum_out=zh_arr[:, 2 * i + 1:2 * i + 2]))
                exp_cks[(idx, 1)] = sAC.ck()
                pump_n(1)
            sDV.wait(ac)
            sAC.inc(ac.activation(ee_arr[:, :], ea_arr[:, :], AF.Exp))
            ee_ck = sAC.ck()
            sAC.wait(ve)
            ve.tensor_mul(tt_arr[:, :], zh_arr[:, :], ee_arr[:, :])
            vd()
            ve.tensor_add(zrow[:, :],
                          tt_arr.rearrange("p (i two) -> p i two", two=2)[:, :, 0],
                          tt_arr.rearrange("p (i two) -> p i two", two=2)[:, :, 1])
            vd()
            ve.tensor_copy(zstat[:, h * 16:(h + 1) * 16], zrow[:, :])
            ve.tensor_reduce(zred[:, :], zrow[:, :], AX, ALU.min)
            vd()
            sDV.inc(ve.tensor_tensor(zmin[:, :], zmin[:, :], zred[:, :],
                                     ALU.min))
            vd()
            sDV.inc(ve.engine_nop())
            merge_ck = sDV.ck()
            head_used[h] = sTE.ck()
    barrier()
    misc_open()

    preduce(sc_col("zmin"), zmin[:, :], ALU.min)
    collective(3, ["zmin"], ["zmin_g"], ALU.min)
    sDV.wait(ve)
    vd()
    ve.reciprocal(sc_col("amax_attn"), sc_col("zmin_g"))
    vd()
    fq_scale("amax_attn", "s_attn", "inv_attn")
    bcast(bc_col("s_attn"), sc_col("s_attn"))
    with nc.sbuf_tensor("btmp", [P, 256], F32) as btmp:
        sDV.wait(ve)
        ve.tensor_scalar(btmp[:, :], zstat[:, :], bc_col("s_attn"), None,
                         ALU.mult)
        sDV.inc(ve.engine_nop())
        sDV.wait(ac)
        sAC.inc(ac.activation(btmp[:, :], btmp[:, :], AF.Ln))
        sAC.wait(ve)
        ve.tensor_scalar(mstat[:, :], mstat[:, :], bc_col("sqsk"), None,
                         ALU.mult)
        vd()
        ve.tensor_add(btmp[:, :], btmp[:, :], mstat[:, :])
        vd()
        sDV.inc(ve.engine_nop())
        sDV.wait(te)
        with nc.sbuf_tensor("btTs", [P, 256], F32) as btTs:
            for j in range(2):
                sTE.inc(te.transpose(misc_ps[:, 0:P],
                                     btmp[:, j * P:(j + 1) * P], eye[:, :]))
                sTE.wait(ve)
                sDV.inc(ve.tensor_copy(btTs[:, j * P:(j + 1) * P],
                                       misc_ps[:, 0:P]))
                sDV.wait(te)
            sDV.wait(gp)
            sMS.dinc(gp.dma_start(
                out=bt_dram.rearrange("(j l) q -> l j q", l=P),
                in_=btTs.rearrange("l (j q) -> l j q", q=P)))
            sMS.wait(gp)
        sGP.inc(gp.engine_nop())
        sGP.wait(sy)
    vd()
    ve.tensor_mul(sc_col("dq_av"), sc_col("s_attn"), sc_col("s_v"))
    vd()
    sDV.inc(ve.engine_nop())
    bcast(bc_col("dq_av"), sc_col("dq_av"))
    barrier()
    if "mz" in dbg:
        sDV.wait(gp)
        sMS.dinc(gp.dma_start(out=dbg_t["mstat"][:, :], in_=mstat[:, :]))
        sMS.dinc(gp.dma_start(out=dbg_t["zstat"][:, :], in_=zstat[:, :]))
        sMS.dinc(gp.dma_start(out=dbg_t["brow"][:, :], in_=bt_dram[:, :]))
        sMS.wait(gp)
        barrier()

    misc_close()
    with nc.sbuf_tensor("brow", [1, 2 * TOWN], F32) as brow, \
         nc.sbuf_tensor("p6q", [P, 2 * TOWN], BF16) as p6q, \
         nc.sbuf_tensor("p6k", [P, 2 * S], BF16) as p6k, \
         nc.sbuf_tensor("p6v", [P, 2 * S], BF16) as p6v, \
         nc.sbuf_tensor("bbc", [P, 2048], F32) as bbc, \
         nc.sbuf_tensor("ein", [P, 4 * 2048], F32) as ein, \
         nc.sbuf_tensor("ei32", [P, 4 * 2048], I32) as ei32, \
         nc.sbuf_tensor("ptb", [P, 4 * 2048], BF16) as ptb, \
         nc.sbuf_tensor("ostg", [P, 2 * 2048], F32) as ostg, \
         nc.sbuf_tensor("ored", [P, 1], F32) as ored, \
         nc.psum_tensor("pl_ps", [P, 4 * 512], F32) as pl_ps, \
         nc.psum_tensor("av_ps", [P, 4 * 512], F32) as av_ps:
        head_used = [None] * NHEADS
        evac_done = [None] * NHEADS
        ost_done = [None] * NHEADS
        stt_all = {}
        exp_all = {}
        cp_all = {}
        av_all = {}
        for h in range(NHEADS):
            qb = p6q[:, (h % 2) * TOWN:(h % 2) * TOWN + TOWN]
            kb = p6k[:, (h % 2) * S:(h % 2) * S + S]
            vb = p6v[:, (h % 2) * S:(h % 2) * S + S]
            if h >= 2:
                wck(sy, head_used[h - 2])
            ld1 = LD.issue(sy, h, qb, nq_sp[h * P:(h + 1) * P, :])
            ld2 = LD2.issue(sy, h, kb, nk_sp[h * P:(h + 1) * P, :])
            ld3 = LD2.issue(
                sy, h, vb.rearrange("p (c m) -> p c m", m=P),
                blk_r(nvtm_bk, h, DM, S))
            browh = brow[0:1, (h % 2) * TOWN:(h % 2) * TOWN + TOWN]
            ldb = LD2.issue(
                sy, h, browh,
                bt_dram[h * 16:(h + 1) * 16, :].rearrange("a b -> (a b)")[None, :])
            for qc in range(4):
                sDV.wait(te)
                wck(te, ldb)
                sTE.inc(te.matmul(
                    pl_ps[:, 0:512], ones_row[:, :],
                    browh[0:1, qc * 512:(qc + 1) * 512],
                    start=True, stop=True))
                sTE.wait(ve)
                sDV.inc(ve.tensor_copy(bbc[:, qc * 512:(qc + 1) * 512],
                                       pl_ps[:, 0:512]))
                sDV.wait(te)
            bias_dv = sDV.ck()
            def emit_av(kt):
                gkt_ = h * 32 + kt
                pbv = ptb[:, (kt % 4) * 2048:(kt % 4) * 2048 + 2048]
                wck(te, cp_all[gkt_])
                wck(te, ld3)
                if kt == 0 and h >= 1:
                    wck(te, evac_done[h - 1])
                for qc in range(4):
                    ins = te.matmul(av_ps[:, qc * 512:(qc + 1) * 512],
                                    vb[:, kt * P:(kt + 1) * P],
                                    pbv[:, qc * 512:(qc + 1) * 512],
                                    start=(kt == 0), stop=(kt == 31))
                sTE.inc(ins)
                av_all[gkt_] = sTE.ck()

            for kt in range(32):
                gkt = h * 32 + kt
                pb = ptb[:, (kt % 4) * 2048:(kt % 4) * 2048 + 2048]
                ei = ei32[:, (kt % 4) * 2048:(kt % 4) * 2048 + 2048]
                einb = ein[:, (kt % 4) * 2048:(kt % 4) * 2048 + 2048]
                for qc in range(4):
                    git = gkt * 4 + qc
                    ps = pl_ps[:, qc * 512:qc * 512 + 512]
                    wck(te, ld1)
                    wck(te, ld2)
                    if kt == 0:
                        wck(te, bias_dv)
                    wck(te, stt_all.get(git - 4))
                    sTE.inc(te.matmul(ps, kb[:, kt * P:(kt + 1) * P],
                                      qb[:, qc * 512:(qc + 1) * 512],
                                      start=True, stop=True))
                    lck = sTE.ck()
                    wck(ve, lck)
                    if qc == 0:
                        wck(ve, exp_all.get(gkt - 4))
                    sDV.inc(ve.scalar_tensor_tensor(
                        einb[:, qc * 512:(qc + 1) * 512], ps, bc_col("sqsk"),
                        bbc[:, qc * 512:(qc + 1) * 512],
                        ALU.mult, ALU.subtract))
                    stt_all[git] = sDV.ck()
                wck(ac, stt_all[git])
                wck(ac, cp_all.get(gkt - 4))
                sAC.inc(ac.activation(ei, einb, AF.Exp))
                exp_all[gkt] = sAC.ck()
                wck(gp, exp_all[gkt])
                if gkt >= 4:
                    wck(gp, av_all.get(gkt - 4))
                cp_all[gkt] = CS.issue(gp, gkt, pb, ei)
                if kt >= 3:
                    emit_av(kt - 3)
            emit_av(29)
            emit_av(30)
            emit_av(31)
            head_used[h] = sTE.ck()
            ostg_h = ostg[:, (h % 2) * 2048:(h % 2) * 2048 + 2048]
            sTE.wait(ac)
            if h >= 2:
                for c_ in ost_done[h - 2]:
                    wck(ac, c_)
            for qc in range(4):
                sAC.inc(ac.activation(ostg_h[:, qc * 512:(qc + 1) * 512],
                                      av_ps[:, qc * 512:(qc + 1) * 512],
                                      AF.Identity, scale=bc_col("dq_av")))
            evac_done[h] = sAC.ck()
            sAC.wait(ve)
            ve.tensor_reduce(ored[:, :], ostg_h, AX, ALU.max,
                             apply_absolute_value=True)
            vd()
            sDV.inc(ve.tensor_tensor(am_col("out"), am_col("out"),
                                     ored[:, :], ALU.max))
            vd()
            dvh = sDV.ck()
            sAC.wait(gp)
            spk = SP.issue(gp, h, out_sp[h * P:(h + 1) * P, :], ostg_h)
            ost_done[h] = [spk, dvh]
        SP.wait_all(gp)
    barrier()
    misc_open()
    if "attnout" in dbg:
        dbg_dump_dram("attnout", out_sp, DM, TOWN, F32)
        barrier()

    preduce(sc_col("amax_out"), am_col("out"), ALU.max)
    collective(4, ["amax_out"], ["gout"], ALU.max)
    sDV.wait(ve)
    fq_scale("gout", "s_out", "inv_out")
    bcast(bc_col("inv_out"), sc_col("inv_out"))
    vd()
    ve.tensor_mul(sc_col("dq_p"), sc_col("s_out"), sc_col("s_wp"))
    vd()
    sDV.inc(ve.engine_nop())
    bcast(bc_col("dq_p"), sc_col("dq_p"))
    barrier()

    st_no = contextlib.ExitStack()
    n_out = st_no.enter_context(nc.sbuf_tensor("n_out", [P, 16 * TOWN], BF16))
    with nc.sbuf_tensor("p7t", [P, 2 * TOWN], F32) as p7t, \
         nc.sbuf_tensor("p7i", [P, TOWN], I32) as p7i:
        ac_done = [None] * 16
        cv_done = [None] * 16
        for i in range(16):
            buf = p7t[:, (i % 2) * TOWN:(i % 2) * TOWN + TOWN]
            if i >= 2:
                wck(sy, ac_done[i - 2])
            ld = LD.issue(sy, i, buf, out_sp[i * P:(i + 1) * P, :])
            wck(ac, ld)
            if i >= 1:
                wck(ac, cv_done[i - 1])
            sAC.inc(ac.activation(p7i[:, :], buf, AF.Identity,
                                  scale=bc_col("inv_out")))
            ac_done[i] = sAC.ck()
            sAC.wait(gp)
            sGP.inc(gp.tensor_copy(n_out[:, i * TOWN:(i + 1) * TOWN],
                                   p7i[:, :]))
            cv_done[i] = sGP.ck()
    barrier()

    with nc.sbuf_tensor("nwb7", [P, 2 * DM], BF16) as nwb7, \
         nc.sbuf_tensor("p7s", [P, 4 * 512], F32) as p7s, \
         nc.sbuf_tensor("x1s", [P, 4 * 512], F32) as x1s, \
         nc.sbuf_tensor("xfh", [P, 2 * 512], F32) as xfh, \
         nc.psum_tensor("proj7_ps", [P, 2 * 512], F32) as proj_ps:
        it = 0
        w_used = {}
        ac_done = {}
        dv_done = {}
        sp_done = {}
        for tc in range(4):
            for oc in range(16):
                if it >= 2:
                    wck(sy, w_used.get(it - 2))
                wbuf = nwb7[:, (it % 2) * DM:(it % 2) * DM + DM]
                ld = LD.issue(
                    sy, it,
                    wbuf.rearrange("p (j m) -> p j m", m=P),
                    blk_r(nw_bk["p"], oc, DM, DM))
                if it >= 2:
                    wck(sy, dv_done.get(it - 2))
                ldx = LD2.issue(sy, it,
                                xfh[:, (it % 2) * 512:(it % 2) * 512 + 512],
                                x_fmh[oc * P:(oc + 1) * P,
                                      tc * 512:(tc + 1) * 512])
                ps = proj_ps[:, (it % 2) * 512:(it % 2) * 512 + 512]
                wck(te, ld)
                if it >= 2:
                    wck(te, ac_done.get(it - 2))
                for j in range(16):
                    ins = te.matmul(ps, wbuf[:, j * P:(j + 1) * P],
                                    n_out[:, j * TOWN + tc * 512:
                                          j * TOWN + tc * 512 + 512],
                                    start=(j == 0), stop=(j == 15))
                sTE.inc(ins)
                stg = p7s[:, (it % 4) * 512:(it % 4) * 512 + 512]
                x1g = x1s[:, (it % 4) * 512:(it % 4) * 512 + 512]
                sTE.wait(ac)
                if it >= 4:
                    wck(ac, sp_done.get(it - 4))
                sAC.inc(ac.activation(stg, ps, AF.Identity,
                                      bias=bias_sb["p"][:, oc:oc + 1],
                                      scale=bc_col("dq_p")))
                ac_done[it] = sAC.ck()
                sAC.wait(ve)
                wck(ve, ldx)
                sDV.inc(ve.tensor_add(
                    x1g, stg, xfh[:, (it % 2) * 512:(it % 2) * 512 + 512]))
                dv_done[it] = sDV.ck()
                sDV.wait(gp)
                sp_done[it] = SP.issue(
                    gp, it,
                    x1_sp[oc * P:(oc + 1) * P, tc * 512:(tc + 1) * 512], x1g)
                w_used[it] = sTE.ck()
                it += 1
        SP.wait_all(gp)
    st_no.close()
    barrier()
    misc_close()
    with nc.sbuf_tensor("x1b", [P, 2 * DM], F32) as x1b, \
         nc.sbuf_tensor("sq7", [P, 2 * DM], F32) as sq7, \
         nc.psum_tensor("st1_ps", [P, 4 * 512], F32) as st1_ps, \
         nc.psum_tensor("st2_ps", [P, 4 * 512], F32) as st2_ps:
        ac7 = {}
        te7 = {}
        for i in range(16):
            buf = x1b[:, (i % 2) * DM:(i % 2) * DM + DM]
            sqb = sq7[:, (i % 2) * DM:(i % 2) * DM + DM]
            if i >= 2:
                wck(sy, te7[i - 2])
            ld = LD.issue(sy, i, buf, x1_sp[i * P:(i + 1) * P, :])
            wck(ac, ld)
            if i >= 2:
                wck(ac, te7[i - 2])
            sAC.inc(ac.activation(sqb, buf, AF.Square))
            ac7[i] = sAC.ck()
            wck(te, ld)
            for tc in range(4):
                te.matmul(st1_ps[0:1, tc * 512:(tc + 1) * 512], ones_col[:, :],
                          buf[:, tc * 512:(tc + 1) * 512],
                          start=(i == 0), stop=(i == 15),
                          skip_group_check=True)
            wck(te, ac7[i])
            for tc in range(4):
                ins = te.matmul(st2_ps[0:1, tc * 512:(tc + 1) * 512],
                                ones_col[:, :],
                                sqb[:, tc * 512:(tc + 1) * 512],
                                start=(i == 0), stop=(i == 15),
                                skip_group_check=True)
            sTE.inc(ins)
            te7[i] = sTE.ck()
        sTE.wait(ve)
        ve.tensor_scalar(srow[0:1, :], st1_ps[0:1, 0:2048], 1.0 / DM,
                         None, ALU.mult)
        ve.tensor_scalar(mrrow[0:1, :], st2_ps[0:1, 0:2048], 1.0 / DM,
                         LN_EPS, ALU.mult, ALU.add)
        vd()
        sDV.inc(ve.engine_nop())
        sDV.wait(te)
    misc_open()
    with nc.sbuf_tensor("m2row", [1, 2048], F32) as m2row:
        ve.tensor_mul(m2row[0:1, :], srow[0:1, :], srow[0:1, :])
        vd()
        ve.tensor_sub(mrrow[0:1, :], mrrow[0:1, :], m2row[0:1, :])
        vd()
        sDV.inc(ve.engine_nop())
        sDV.wait(ac)
        sAC.inc(ac.activation(m2row[0:1, :], mrrow[0:1, :], AF.Sqrt))
        sAC.wait(ve)
        ve.reciprocal(m2row[0:1, :], m2row[0:1, :])
        vd()
        ve.tensor_mul(mrrow[0:1, :], srow[0:1, :], m2row[0:1, :])
        ve.tensor_copy(srow[0:1, :], m2row[0:1, :])
        vd()
        sDV.inc(ve.engine_nop())
    barrier()
    if "x1" in dbg:
        dbg_dump_dram("x1", x1_sp, DM, TOWN, F32)
        barrier()

    with nc.sbuf_tensor("x1l", [P, 16 * 512], BF16) as x1l, \
         nc.sbuf_tensor("hid", [P, 64 * 512], BF16) as hid, \
         nc.sbuf_tensor("rbc", [P, 512], F32) as rbc, \
         nc.sbuf_tensor("mrbc", [P, 512], F32) as mrbc, \
         nc.sbuf_tensor("x1c", [P, 2 * 512], F32) as x1c, \
         nc.sbuf_tensor("ugw", [P, 2 * 2 * DM], BF16) as ugw, \
         nc.sbuf_tensor("dww", [P, 2 * DE], BF16) as dww, \
         nc.sbuf_tensor("silu", [P, 2 * 512], F32) as silu_t, \
         nc.sbuf_tensor("fstg", [P, 2 * 512], F32) as fstg, \
         nc.psum_tensor("ffn_ps", [P, 6 * 512], F32) as ffn_ps:
        for name in ("u", "g", "d"):
            bcast(bc_col("tscale" + name), sc_col("tscale" + name))
        s1_used = None
        s2_hid_used = None
        stt_hid_last = None
        stt_out_last2 = [None, None]
        p9_prev_dv = None
        for tc in range(4):
            sDV.wait(te)
            sTE.inc(te.matmul(misc_ps[:, 0:512], ones_row[:, :],
                              srow[0:1, tc * 512:(tc + 1) * 512],
                              start=True, stop=True))
            sTE.wait(ve)
            if s1_used is not None:
                wck(ve, s1_used)
            sDV.inc(ve.tensor_copy(rbc[:, :], misc_ps[:, 0:512]))
            sDV.wait(te)
            sTE.inc(te.matmul(misc_ps[:, 0:512], ones_row[:, :],
                              mrrow[0:1, tc * 512:(tc + 1) * 512],
                              start=True, stop=True))
            sTE.wait(ve)
            sDV.inc(ve.tensor_copy(mrbc[:, :], misc_ps[:, 0:512]))
            sDV.wait(te)
            dv_done = [None] * 16
            for oc in range(16):
                cbuf = x1c[:, (oc % 2) * 512:(oc % 2) * 512 + 512]
                if oc >= 2:
                    wck(sy, dv_done[oc - 2])
                elif p9_prev_dv is not None:
                    wck(sy, p9_prev_dv)
                ld = LD.issue(sy, oc, cbuf,
                              x1_sp[oc * P:(oc + 1) * P,
                                    tc * 512:(tc + 1) * 512])
                wck(ve, ld)
                ve.tensor_mul(cbuf, cbuf, rbc[:, :])
                sDV.inc(ve.tensor_sub(x1l[:, oc * 512:(oc + 1) * 512],
                                      cbuf, mrbc[:, :]))
                dv_done[oc] = sDV.ck()
            x1l_ready = sDV.ck()
            ug_used = dict(ug_prev) if tc > 0 else {}
            stt_hid = {}
            for ec in range(64):
                ub = ugw[:, (ec % 2) * 2 * DM:(ec % 2) * 2 * DM + DM]
                gb2 = ugw[:, (ec % 2) * 2 * DM + DM:(ec % 2) * 2 * DM + 2 * DM]
                if ec >= 2:
                    wck(sy, ug_used[ec - 2])
                elif tc > 0:
                    wck(sy, ug_used[62 + ec])
                lu = LD.issue(
                    sy, ec, ub.rearrange("p (j m) -> p j m", m=P),
                    blk_r(tu_bk, ec, DE, DM))
                lg = LD2.issue(
                    sy, ec, gb2.rearrange("p (j m) -> p j m", m=P),
                    blk_r(tg_bk, ec, DE, DM))
                s1b = (ec % 2) * 1024
                wck(te, lu)
                wck(te, lg)
                if ec == 0:
                    for j_ in range(16):
                        wck(te, dv_done[j_])
                else:
                    wck(te, x1l_ready)
                if ec >= 2:
                    wck(te, stt_hid[ec - 2])
                elif stt_hid_last is not None:
                    wck(te, stt_hid_last)
                if ec <= 1 and s2_hid_used is not None:
                    wck(te, s2_hid_used)
                for j in range(16):
                    te.matmul(ffn_ps[:, s1b:s1b + 512],
                              ub[:, j * P:(j + 1) * P],
                              x1l[:, j * 512:(j + 1) * 512],
                              start=(j == 0), stop=(j == 15))
                for j in range(16):
                    i2 = te.matmul(ffn_ps[:, s1b + 512:s1b + 1024],
                                   gb2[:, j * P:(j + 1) * P],
                                   x1l[:, j * 512:(j + 1) * 512],
                                   start=(j == 0), stop=(j == 15))
                sTE.inc(i2)
                ug_used[ec] = sTE.ck()
                wck(ac, ug_used[ec])
                if ec >= 2:
                    wck(ac, stt_hid[ec - 2])
                elif stt_hid_last is not None:
                    wck(ac, stt_hid_last)
                sAC.inc(ac.activation(silu_t[:, (ec % 2) * 512:
                                              (ec % 2) * 512 + 512],
                                      ffn_ps[:, s1b:s1b + 512], AF.Silu,
                                      scale=bc_col("tscaleu")))
                sil_ck = sAC.ck()
                wck(ve, sil_ck)
                sDV.inc(ve.scalar_tensor_tensor(
                    hid[:, ec * 512:(ec + 1) * 512],
                    ffn_ps[:, s1b + 512:s1b + 1024],
                    bc_col("tscaleg"),
                    silu_t[:, (ec % 2) * 512:(ec % 2) * 512 + 512],
                    ALU.mult, ALU.mult))
                stt_hid[ec] = sDV.ck()
            s1_used = sTE.ck()
            ug_prev = {62: ug_used[62], 63: ug_used[63]}
            stt_hid_last = stt_hid[63]
            hid_ready = sDV.ck()
            dw_used = dict(dw_prev) if tc > 0 else {}
            stt_out = {}
            sp_out = {}
            for oc in range(16):
                db = dww[:, (oc % 2) * DE:(oc % 2) * DE + DE]
                if oc >= 2:
                    wck(sy, dw_used[oc - 2])
                elif tc > 0:
                    wck(sy, dw_used[14 + oc])
                ldw = LD.issue(
                    sy, oc, db.rearrange("p (j m) -> p j m", m=P),
                    blk_r(td_bk, oc, DM, DE))
                if oc >= 2:
                    wck(sy, stt_out[oc - 2])
                elif x1l_ready is not None:
                    wck(sy, x1l_ready)
                ldx = LD2.issue(sy, oc,
                                x1c[:, (oc % 2) * 512:(oc % 2) * 512 + 512],
                                x1_sp[oc * P:(oc + 1) * P,
                                      tc * 512:(tc + 1) * 512])
                ps = ffn_ps[:, (4 + oc % 2) * 512:(4 + oc % 2) * 512 + 512]
                wck(te, ldw)
                wck(te, hid_ready)
                if oc >= 2:
                    wck(te, stt_out[oc - 2])
                elif stt_out_last2[oc] is not None:
                    wck(te, stt_out_last2[oc])
                for j in range(64):
                    ins = te.matmul(ps, db[:, j * P:(j + 1) * P],
                                    hid[:, j * 512:(j + 1) * 512],
                                    start=(j == 0), stop=(j == 63))
                sTE.inc(ins)
                dw_used[oc] = sTE.ck()
                stg = fstg[:, (oc % 2) * 512:(oc % 2) * 512 + 512]
                sTE.wait(ve)
                wck(ve, ldx)
                if oc >= 2:
                    wck(ve, sp_out[oc - 2])
                sDV.inc(ve.scalar_tensor_tensor(
                    stg, ps, bc_col("tscaled"),
                    x1c[:, (oc % 2) * 512:(oc % 2) * 512 + 512],
                    ALU.mult, ALU.add))
                stt_out[oc] = sDV.ck()
                sDV.wait(gp)
                sp_out[oc] = SP.issue(
                    gp, oc,
                    out_fm[oc * P:(oc + 1) * P, tc * 512:(tc + 1) * 512], stg)
            s2_hid_used = sTE.ck()
            dw_prev = {14: dw_used[14], 15: dw_used[15]}
            stt_out_last2 = [stt_out[14], stt_out[15]]
            p9_prev_dv = sDV.ck()
            SP.wait_all(gp)
        sGP.inc(gp.engine_nop())
        sGP.wait(sy)
        sGP.wait(te)
    if "nwq" in dbg:
        dbg_dump_dram("nwq", nw_bk["q"], DM * 16, P, BF16)
        barrier()
    if "sc" in dbg:
        sMS.dinc(gp.dma_start(out=dbg_t["sc"][0:1, :], in_=sc[0:1, :]))
        sMS.wait(gp)
        barrier()
    st.close()
    return nc


DBG_SHAPES = {
    "sc": ([1, 96], F32),
    "nwq": ([DM * 16, P], BF16),
    "mstat": ([P, 256], F32),
    "zstat": ([P, 256], F32),
    "brow": ([256, P], F32),
    "nh": ([S, DM], BF16),
    "q": ([DM, TOWN], F32),
    "k": ([DM, S], F32),
    "v": ([DM, S], F32),
    "attnout": ([DM, TOWN], F32),
    "x1": ([DM, TOWN], F32),
}


def make_in_maps(inputs):
    x = np.ascontiguousarray(inputs["L0_attn_hidden_states"], np.float32)
    shared = {}
    for k, key in (("q", "L0_attn_Wq"), ("k", "L0_attn_Wk"),
                   ("v", "L0_attn_Wv"), ("p", "L0_attn_Wp")):
        shared[f"w{k}_t"] = np.ascontiguousarray(
            np.asarray(inputs[key], np.float32).T)
    for k, key in (("q", "L0_attn_bq"), ("k", "L0_attn_bk"),
                   ("v", "L0_attn_bv"), ("p", "L0_attn_bp")):
        shared[f"b{k}2d"] = np.ascontiguousarray(
            np.asarray(inputs[key], np.float32).reshape(16, P).T)
    shared["wu_t"] = np.ascontiguousarray(
        np.asarray(inputs["L0_ffn_W_up"], np.float32).T)
    shared["wg_t"] = np.ascontiguousarray(
        np.asarray(inputs["L0_ffn_W_gate"], np.float32).T)
    shared["wd_t"] = np.ascontiguousarray(
        np.asarray(inputs["L0_ffn_W_down"], np.float32).T)
    shared["eye128"] = np.eye(P, dtype=np.float32)
    maps = []
    for c in range(N_CORES):
        b, half = c // 2, c % 2
        xb = np.asarray(x[b], np.float32)
        x_tok_ = np.roll(xb, -half * TOWN, axis=0)
        m = {"x_tok": np.ascontiguousarray(x_tok_),
             "x_fmh": np.ascontiguousarray(x_tok_[:TOWN].T)}
        m.update(shared)
        maps.append(m)
    return maps


def kernel(**inputs):
    nc = build_program()
    maps = make_in_maps(inputs)
    res = run_bass_kernel_spmd(nc, maps, core_ids=list(range(N_CORES)))
    out = np.empty((4, S, DM), np.float32)
    for c in range(N_CORES):
        b, half = c // 2, c % 2
        shard = np.asarray(res.results[c]["out_fm"])
        out[b, half * TOWN:(half + 1) * TOWN, :] = shard.T
    return out


# version 17
# speedup vs baseline: 1.3678x; 1.3678x over previous
import contextlib
import numpy as np

import concourse.bass as bass
import concourse.mybir as mybir
from concourse.bass_utils import run_bass_kernel_spmd

F32 = mybir.dt.float32
BF16 = mybir.dt.bfloat16
I32 = mybir.dt.int32
AF = mybir.ActivationFunctionType
ALU = mybir.AluOpType
AX = mybir.AxisListType.X

P = 128
DM = 2048
S = 4096
TOWN = 2048
NHEADS = 16
DE = 8192
QMUL = float(np.float32(1.0 / np.sqrt(128.0)))
LN_EPS = 1e-5
FQ_EPS = 1e-8
TERN_EPS = 1e-8
FBIG = 3.0e38

N_CORES = 8
REPL = [list(range(N_CORES))]


class Sem:
    def __init__(self, stack, nc, name):
        self.h = stack.enter_context(nc.semaphore(name))
        self.v = 0

    def inc(self, instr, n=1):
        instr.then_inc(self.h, n)
        self.v += n
        return instr

    def dinc(self, instr):
        return self.inc(instr, 16)

    def ck(self):
        return (self, self.v)

    def wait(self, eng, v=None):
        eng.wait_ge(self.h, self.v if v is None else v)


def wck(eng, ck):
    if ck is not None:
        sem, v = ck
        eng.wait_ge(sem.h, v)


class DmaSlots:

    def __init__(self, stack, nc, name, n=2):
        self.sems = [Sem(stack, nc, f"{name}{d}") for d in range(n)]
        self.n = n

    def issue(self, eng, i, out, in_, transpose=False):
        s = self.sems[i % self.n]
        if transpose:
            s.dinc(eng.dma_start_transpose(out, in_))
        else:
            s.dinc(eng.dma_start(out=out, in_=in_))
        return s.ck()

    def wait_all(self, eng):
        for s in self.sems:
            s.wait(eng)


def build_program(dbg=()):
    nc = bass.Bass("TRN2", target_bir_lowering=False)
    st = contextlib.ExitStack()

    x_tok = nc.dram_tensor("x_tok", [S, DM], F32, kind="ExternalInput").ap()
    x_fmh = nc.dram_tensor("x_fmh", [DM, TOWN], F32, kind="ExternalInput").ap()
    w_t = {k: nc.dram_tensor(f"w{k}_t", [DM, DM], F32, kind="ExternalInput").ap()
           for k in "qkvp"}
    b2d = {k: nc.dram_tensor(f"b{k}2d", [P, 16], F32, kind="ExternalInput").ap()
           for k in "qkvp"}
    wu_t = nc.dram_tensor("wu_t", [DM, DE], F32, kind="ExternalInput").ap()
    wg_t = nc.dram_tensor("wg_t", [DM, DE], F32, kind="ExternalInput").ap()
    wd_t = nc.dram_tensor("wd_t", [DE, DM], F32, kind="ExternalInput").ap()
    eye_in = nc.dram_tensor("eye128", [P, P], F32, kind="ExternalInput").ap()
    out_fm = nc.dram_tensor("out_fm", [DM, TOWN], F32, kind="ExternalOutput").ap()

    dscr = lambda name, shape, dt: nc.dram_tensor(name, shape, dt).ap()
    nh_sp = dscr("nh_sp", [S, DM], BF16)
    q_sp = dscr("q_sp", [DM, TOWN], F32)
    k_sp = dscr("k_sp", [DM, S], F32)
    v_sp = dscr("v_sp", [DM, S], F32)
    nq_sp = dscr("nq_sp", [DM, TOWN], BF16)
    nk_sp = dscr("nk_sp", [DM, S], BF16)
    nv_sp = dscr("nv_sp", [DM, S], BF16)
    nvtm_bk = dscr("nvtm_bk", [S * (DM // P), P], BF16)
    nw_bk = {k: dscr(f"nw{k}_bk", [DM * (DM // P), P], BF16) for k in "qkvp"}
    tu_bk = dscr("tu_bk", [DM * (DE // P), P], BF16)
    tg_bk = dscr("tg_bk", [DM * (DE // P), P], BF16)
    td_bk = dscr("td_bk", [DE * (DM // P), P], BF16)
    out_sp = dscr("out_sp", [DM, TOWN], F32)
    x1_sp = dscr("x1_sp", [DM, TOWN], F32)
    bt_dram = dscr("bt_dram", [256, P], F32)
    cc_in = {i: dscr(f"cc{i}_in", [1, 4], F32) for i in range(1, 5)}
    cc_out = {i: dscr(f"cc{i}_out", [1, 4], F32) for i in range(1, 5)}

    def blk_w(blk, rc, ncols):
        return blk[rc * ncols:(rc + 1) * ncols, :]

    def blk_r(blk, cc, ncols, nrows):
        return blk.rearrange("(r c p) m -> c p r m", c=ncols // P, p=P)[cc]

    dbg_t = {}
    _dbgnames = []
    for name in dbg:
        _dbgnames += (["q", "k", "v"] if name == "qkv" else
                      ["mstat", "zstat", "brow"] if name == "mz" else [name])
    for name in _dbgnames:
        shape, dt = DBG_SHAPES[name]
        dbg_t[name] = nc.dram_tensor("dbg_" + name, shape, dt,
                                     kind="ExternalOutput").ap()

    sb = lambda name, shape, dt: st.enter_context(nc.sbuf_tensor(name, shape, dt))
    ones_row = sb("ones_row", [1, P], F32)
    ones_col = sb("ones_col", [P, 1], F32)
    eye = sb("eye", [P, P], F32)
    bias_sb = {k: sb(f"bias_{k}", [P, 16], F32) for k in "qkvp"}
    ln_m = sb("ln_m", [P, 32], F32)
    ln_r = sb("ln_r", [P, 32], F32)
    ln_a = sb("ln_a", [P, 32], F32)
    ln_b = sb("ln_b", [P, 32], F32)
    mstat = sb("mstat", [P, 256], F32)
    zstat = sb("zstat", [P, 256], F32)
    BCN = 32
    bc = sb("bc", [P, BCN], F32)
    BC = {}
    sc = sb("sc", [1, 96], F32)
    SC = {}
    SC_STAGE = 88
    amax = sb("amax", [P, 12], F32)
    AM = {}
    zmin = sb("zmin", [P, 1], F32)
    srow = sb("srow", [1, 2048], F32)
    mrrow = sb("mrrow", [1, 2048], F32)

    def bc_col(name):
        if name not in BC:
            assert len(BC) < BCN, "bc overflow"
            BC[name] = bc[:, len(BC):len(BC) + 1]
        return BC[name]

    def sc_col(name):
        if name not in SC:
            assert len(SC) < SC_STAGE, "sc overflow"
            SC[name] = sc[:, len(SC):len(SC) + 1]
        return SC[name]

    def am_col(name):
        if name not in AM:
            assert len(AM) < 12, "amax overflow"
            AM[name] = amax[:, len(AM):len(AM) + 1]
        return AM[name]

    te, ve, ac, gp, sy = nc.tensor, nc.vector, nc.scalar, nc.gpsimd, nc.sync
    LD = DmaSlots(st, nc, "LD", 2)
    LD2 = DmaSlots(st, nc, "LE", 2)
    SP = DmaSlots(st, nc, "SP", 4)
    CS = DmaSlots(st, nc, "CS", 4)
    F1 = DmaSlots(st, nc, "F1", 2)
    F2 = DmaSlots(st, nc, "F2", 2)
    sTE = Sem(st, nc, "sTE")
    sDV = Sem(st, nc, "sDV")
    sAC = Sem(st, nc, "sAC")
    sGP = Sem(st, nc, "sGP")
    sCC = Sem(st, nc, "sCC")
    sMS = Sem(st, nc, "sMS")

    def barrier():
        nc.all_engine_barrier()

    def vd():
        ve.drain()

    _mps = {"stack": None, "t": None, "n": 0}

    def misc_open():
        _mps["stack"] = contextlib.ExitStack()
        _mps["n"] += 1
        _mps["t"] = _mps["stack"].enter_context(
            nc.psum_tensor(f"misc_ps{_mps['n']}", [P, 512], F32))

    def misc_close():
        _mps["stack"].close()

    misc_open()

    class _MiscPs:
        def __getitem__(self, sl):
            return _mps["t"][sl]

    misc_ps = _MiscPs()

    def bcast(dst_col, src11):
        sDV.wait(te)
        sTE.inc(te.matmul(misc_ps[:, 0:1], ones_row[:, :], src11,
                          start=True, stop=True))
        sTE.wait(ve)
        sDV.inc(ve.tensor_copy(dst_col, misc_ps[:, 0:1]))
        sDV.wait(te)
        sDV.wait(ac)
        sDV.wait(gp)

    def preduce(dst11, src_col, op):
        sDV.wait(te)
        sTE.inc(te.transpose(misc_ps[0:1, 0:P], src_col, eye[:, :]))
        sTE.wait(ve)
        sDV.inc(ve.tensor_reduce(dst11, misc_ps[0:1, 0:P], AX, op))
        sDV.wait(te)
        vd()

    def fq_scale(name_amax, name_s, name_inv):
        vd()
        ve.tensor_scalar(sc_col(name_s), sc_col(name_amax), 1.0 / 127.0,
                         FQ_EPS, ALU.mult, ALU.max)
        vd()
        sDV.inc(ve.reciprocal(sc_col(name_inv), sc_col(name_s)))
        vd()

    def collective(idx, names_in, names_out, op):
        n = len(names_in)
        sDV.wait(ve)
        vd()
        for j, nm in enumerate(names_in):
            ve.tensor_copy(sc[0:1, SC_STAGE + j:SC_STAGE + j + 1], sc_col(nm))
        sDV.inc(ve.engine_nop())
        sDV.wait(gp)
        sMS.dinc(gp.dma_start(out=cc_in[idx][0:1, 0:n],
                              in_=sc[0:1, SC_STAGE:SC_STAGE + n]))
        sMS.wait(gp)
        sCC.inc(gp.collective_compute(
            "AllReduce", op, replica_groups=REPL,
            ins=[cc_in[idx][0:1, 0:n]], outs=[cc_out[idx][0:1, 0:n]]))
        sCC.wait(gp)
        sMS.dinc(gp.dma_start(out=sc[0:1, SC_STAGE:SC_STAGE + n],
                              in_=cc_out[idx][0:1, 0:n]))
        sMS.wait(gp)
        sGP.inc(gp.engine_nop())
        sGP.wait(ve)
        for j, nm in enumerate(names_out):
            ve.tensor_copy(sc_col(nm), sc[0:1, SC_STAGE + j:SC_STAGE + j + 1])
        sDV.inc(ve.engine_nop())
        vd()

    def dbg_dump_dram(name, src_ap, rows, cols, dt):
        with nc.sbuf_tensor("dbgb_" + name, [P, cols], dt) as t:
            for i in range(rows // P):
                sMS.dinc(gp.dma_start(out=t[:, :],
                                      in_=src_ap[i * P:(i + 1) * P, :]))
                sMS.wait(gp)
                sMS.dinc(gp.dma_start(out=dbg_t[name][i * P:(i + 1) * P, :],
                                      in_=t[:, :]))
                sMS.wait(gp)

    ve.memset(ones_row[:, :], 1.0)
    ve.memset(ones_col[:, :], 1.0)
    ve.memset(zmin[:, :], FBIG)
    ve.memset(amax[:, :], 0.0)
    sMS.dinc(sy.dma_start(out=eye[:, :], in_=eye_in[:, :]))
    for k in "qkvp":
        sMS.dinc(sy.dma_start(out=bias_sb[k][:, :], in_=b2d[k][:, :]))
    sMS.wait(sy)
    barrier()

    NT = S // P
    with nc.sbuf_tensor("xa", [P, 4 * DM], F32) as xa, \
         nc.sbuf_tensor("sq_a", [P, DM], F32) as sq_a, \
         nc.sbuf_tensor("sq_b", [P, DM], F32) as sq_b, \
         nc.sbuf_tensor("p1tmp", [P, 12], F32) as tp:
        s1 = tp[:, 0:1]; mm_ = tp[:, 2:3]; m2 = tp[:, 3:4]
        vv = tp[:, 4:5]; sd = tp[:, 5:6]; rr = tp[:, 6:7]; hx = tp[:, 7:8]
        dv_done = [None] * NT
        ac_done = [None] * NT
        for i in range(NT):
            buf = xa[:, (i % 4) * DM:(i % 4) * DM + DM]
            s2 = tp[:, 8 + i % 2:9 + i % 2]
            if i >= 4:
                wck(sy, dv_done[i - 4])
                wck(sy, ac_done[i - 4])
            ld = LD.issue(sy, i, buf, x_tok[i * P:(i + 1) * P, :])
            wck(ve, ld)
            ve.reduce_sum(s1, buf, AX)
            wck(ac, ld)
            sAC.inc(ac.activation(sq_a[:, :], buf, AF.Square, accum_out=s2))
            ac_done[i] = sAC.ck()
            vd()
            ve.tensor_scalar(mm_, s1, 1.0 / DM, None, ALU.mult)
            sAC.wait(ve)
            vd()
            ve.tensor_mul(m2, mm_, mm_)
            ve.tensor_scalar(vv, s2, 1.0 / DM, LN_EPS, ALU.mult, ALU.add)
            vd()
            ve.tensor_sub(vv, vv, m2)
            ve.tensor_copy(ln_m[:, i:i + 1], mm_)
            sDV.inc(ve.engine_nop())
            sDV.wait(ac)
            sAC.inc(ac.activation(sd, vv, AF.Sqrt))
            sAC.wait(ve)
            ve.reciprocal(rr, sd)
            vd()
            ve.tensor_copy(ln_r[:, i:i + 1], rr)
            ve.tensor_scalar(sq_b[:, :], buf, mm_, None, ALU.subtract)
            ve.tensor_reduce(hx, sq_b[:, :], AX, ALU.max,
                             apply_absolute_value=True)
            vd()
            ve.tensor_mul(hx, hx, rr)
            vd()
            sDV.inc(ve.tensor_tensor(am_col("h"), am_col("h"), hx, ALU.max))
            dv_done[i] = sDV.ck()
    preduce(sc_col("amax_h"), am_col("h"), ALU.max)
    collective(1, ["amax_h"], ["amax_h_g"], ALU.max)
    sDV.wait(ve)
    fq_scale("amax_h_g", "s_h", "inv_sh")
    bcast(bc_col("inv_sh"), sc_col("inv_sh"))
    sDV.wait(ve)
    ve.tensor_scalar(ln_a[:, :], ln_r[:, :], bc_col("inv_sh"), None, ALU.mult)
    vd()
    ve.tensor_mul(ln_b[:, :], ln_m[:, :], ln_a[:, :])
    vd()
    sDV.inc(ve.tensor_scalar(ln_b[:, :], ln_b[:, :], -1.0, None, ALU.mult))
    vd()
    barrier()

    def quant_stream(src, ntile, fd, scale_cols, bias_cols=None, clip=False,
                     xbuf=None, ibuf=None, obuf2=None, dst=None, dst_blk=None,
                     nbuf=2):
        ac_done = [None] * ntile
        cv_done = [None] * ntile
        sp_done = [None] * ntile
        for i in range(ntile):
            buf = xbuf[:, (i % nbuf) * fd:(i % nbuf) * fd + fd]
            if i >= nbuf:
                wck(sy, ac_done[i - nbuf])
            ld = LD.issue(sy, i, buf, src[i * P:(i + 1) * P, :])
            wck(ac, ld)
            if i >= nbuf:
                wck(ac, cv_done[i - nbuf])
            kw = {}
            if scale_cols is not None:
                kw["scale"] = scale_cols[i] if isinstance(scale_cols, list) \
                    else scale_cols
            if bias_cols is not None:
                kw["bias"] = bias_cols[i]
            ib = ibuf[:, (i % nbuf) * fd:(i % nbuf) * fd + fd]
            sAC.inc(ac.activation(ib, buf, AF.Identity, **kw))
            ac_done[i] = sAC.ck()
            if clip:
                ob = obuf2[:, (i % 2) * fd:(i % 2) * fd + fd]
                assert nbuf == 2
                if i >= 2:
                    wck(ve, sp_done[i - 2])
                sAC.wait(ve)
                sDV.inc(ve.tensor_scalar(ob, ib, -1.0, 1.0,
                                         ALU.max, ALU.min))
                cv_done[i] = sDV.ck()
                sDV.wait(gp)
                if dst_blk is not None:
                    sp_done[i] = SP.issue(
                        gp, i,
                        blk_w(dst_blk, i, fd).rearrange("(c p) m -> p c m",
                                                        p=P),
                        ob.rearrange("p (c m) -> p c m", m=P))
                else:
                    sp_done[i] = SP.issue(gp, i, dst[i * P:(i + 1) * P, :], ob)
            else:
                sAC.wait(gp)
                if dst_blk is not None:
                    ck2 = SP.issue(
                        gp, i,
                        blk_w(dst_blk, i, fd).rearrange("(c p) m -> p c m",
                                                        p=P),
                        ib.rearrange("p (c m) -> p c m", m=P))
                else:
                    ck2 = SP.issue(gp, i, dst[i * P:(i + 1) * P, :], ib)
                cv_done[i] = ck2
                sp_done[i] = ck2
        SP.wait_all(gp)

    with nc.sbuf_tensor("xb", [P, 4 * DM], F32) as xb, \
         nc.sbuf_tensor("nhi", [P, 4 * DM], I32) as nhi:
        quant_stream(x_tok, NT, DM,
                     [ln_a[:, i:i + 1] for i in range(NT)],
                     bias_cols=[ln_b[:, i:i + 1] for i in range(NT)],
                     xbuf=xb, ibuf=nhi, dst=nh_sp, nbuf=4)
    barrier()
    if "nh" in dbg:
        dbg_dump_dram("nh", nh_sp, S, DM, BF16)
        barrier()

    with nc.sbuf_tensor("ft", [P, 3 * DE], F32) as ft, \
         nc.sbuf_tensor("fred", [P, 1], F32) as fred, \
         nc.sbuf_tensor("fsum", [P, 1], F32) as fsum:
        for (w_in, w_bk, nrows, name) in ((wu_t, tu_bk, DM, "u"),
                                          (wg_t, tg_bk, DM, "g"),
                                          (wd_t, td_bk, DE, "d")):
            ntile = nrows // P
            fd = w_in.shape[1]
            ve.memset(fsum[:, :], 0.0)
            dv_done = [None] * ntile
            for i in range(ntile):
                buf = ft[:, (i % 3) * DE:(i % 3) * DE + fd]
                if i >= 3:
                    wck(sy, dv_done[i - 3])
                ld = LD.issue(sy, i, buf, w_in[i * P:(i + 1) * P, :])
                wck(ve, ld)
                ve.tensor_reduce(fred[:, :], buf, AX, ALU.add,
                                 apply_absolute_value=True)
                vd()
                sDV.inc(ve.tensor_add(fsum[:, :], fsum[:, :], fred[:, :]))
                dv_done[i] = sDV.ck()
            preduce(sc_col("tsum" + name), fsum[:, :], ALU.add)
            ve.tensor_scalar(sc_col("tscale" + name), sc_col("tsum" + name),
                             1.0 / (DM * DE), TERN_EPS, ALU.mult, ALU.add)
            vd()
            sDV.inc(ve.reciprocal(sc_col("tinv" + name),
                                  sc_col("tscale" + name)))
            vd()
            bcast(bc_col("tinv" + name), sc_col("tinv" + name))
            barrier()
    barrier()


    st_nh = contextlib.ExitStack()
    nh_fm = st_nh.enter_context(nc.sbuf_tensor("nh_fm", [P, 16 * S], BF16))
    for j in range(16):
        LD.issue(sy, j, nh_fm[:, j * S:(j + 1) * S],
                 nh_sp[:, j * P:(j + 1) * P], transpose=True)
    LD.wait_all(sy)
    barrier()

    with nc.sbuf_tensor("wt", [P, 2 * DM], F32) as wt, \
         nc.sbuf_tensor("wi", [P, 2 * DM], I32) as wi, \
         nc.sbuf_tensor("wred", [P, 1], F32) as wred:
        for k in "qkvp":
            dv_done = [None] * 16
            for i in range(16):
                buf = wt[:, (i % 2) * DM:(i % 2) * DM + DM]
                if i >= 2:
                    wck(sy, dv_done[i - 2])
                ld = LD.issue(sy, i, buf, w_t[k][i * P:(i + 1) * P, :])
                wck(ve, ld)
                ve.tensor_reduce(wred[:, :], buf, AX, ALU.max,
                                 apply_absolute_value=True)
                vd()
                sDV.inc(ve.tensor_tensor(am_col("w" + k), am_col("w" + k),
                                         wred[:, :], ALU.max))
                dv_done[i] = sDV.ck()
            preduce(sc_col("amax_w" + k), am_col("w" + k), ALU.max)
            fq_scale("amax_w" + k, "s_w" + k, "inv_w" + k)
            bcast(bc_col("inv_w" + k), sc_col("inv_w" + k))
            barrier()
            quant_stream(w_t[k], 16, DM, bc_col("inv_w" + k),
                         xbuf=wt, ibuf=wi, dst_blk=nw_bk[k])
            barrier()
    barrier()

    vd()
    for k in "qkv":
        ve.tensor_mul(sc_col("dq_" + k), sc_col("s_h"), sc_col("s_w" + k))
    vd()
    ve.tensor_scalar(sc_col("dq_q"), sc_col("dq_q"), QMUL, None, ALU.mult)
    ve.tensor_scalar(bias_sb["q"][:, :], bias_sb["q"][:, :], QMUL, None,
                     ALU.mult)
    vd()
    sDV.inc(ve.engine_nop())
    for k in "qkv":
        bcast(bc_col("dq_" + k), sc_col("dq_" + k))
    barrier()

    with nc.sbuf_tensor("nwb", [P, 2 * DM], BF16) as nwb, \
         nc.sbuf_tensor("pstg", [P, 4 * 512], F32) as pstg, \
         nc.sbuf_tensor("pred", [P, 1], F32) as pred, \
         nc.psum_tensor("proj_ps", [P, 2 * 512], F32) as proj_ps:
        for k, tok, spill in (("q", TOWN, q_sp), ("k", S, k_sp), ("v", S, v_sp)):
            if k != "q":
                SP.wait_all(gp)
                barrier()
            ntc = tok // 512
            it = 0
            w_used = [None] * 16
            ac_done = {}
            dv_done = {}
            sp_done = {}
            for oc in range(16):
                if oc >= 2:
                    wck(sy, w_used[oc - 2])
                wbuf = nwb[:, (oc % 2) * DM:(oc % 2) * DM + DM]
                ld = LD.issue(
                    sy, oc,
                    wbuf.rearrange("p (j m) -> p j m", m=P),
                    blk_r(nw_bk[k], oc, DM, DM))
                for tc in range(ntc):
                    ps = proj_ps[:, (it % 2) * 512:(it % 2) * 512 + 512]
                    wck(te, ld)
                    if it >= 2:
                        wck(te, ac_done.get(it - 2))
                    for j in range(16):
                        ins = te.matmul(
                            ps, wbuf[:, j * P:(j + 1) * P],
                            nh_fm[:, j * S + tc * 512:j * S + tc * 512 + 512],
                            start=(j == 0), stop=(j == 15))
                    sTE.inc(ins)
                    stg = pstg[:, (it % 4) * 512:(it % 4) * 512 + 512]
                    sTE.wait(ac)
                    if it >= 4:
                        wck(ac, sp_done.get(it - 4))
                        wck(ac, dv_done.get(it - 4))
                    sAC.inc(ac.activation(stg, ps, AF.Identity,
                                          bias=bias_sb[k][:, oc:oc + 1],
                                          scale=bc_col("dq_" + k)))
                    ac_done[it] = sAC.ck()
                    sAC.wait(ve)
                    ve.tensor_reduce(pred[:, :], stg, AX, ALU.max,
                                     apply_absolute_value=True)
                    vd()
                    sDV.inc(ve.tensor_tensor(am_col(k), am_col(k),
                                             pred[:, :], ALU.max))
                    dv_done[it] = sDV.ck()
                    sAC.wait(gp)
                    sp_done[it] = SP.issue(
                        gp, it,
                        spill[oc * P:(oc + 1) * P, tc * 512:tc * 512 + 512],
                        stg)
                    it += 1
                w_used[oc] = sTE.ck()
        SP.wait_all(gp)
    st_nh.close()
    barrier()
    if "qkv" in dbg:
        dbg_dump_dram("q", q_sp, DM, TOWN, F32)
        dbg_dump_dram("k", k_sp, DM, S, F32)
        dbg_dump_dram("v", v_sp, DM, S, F32)
        barrier()

    for k in "qkv":
        preduce(sc_col("amax_" + k), am_col(k), ALU.max)
    collective(2, ["amax_q", "amax_k", "amax_v"], ["gq", "gk", "gv"], ALU.max)
    sDV.wait(ve)
    for k in "qkv":
        fq_scale("g" + k, "s_" + k, "inv_" + k)
        bcast(bc_col("inv_" + k), sc_col("inv_" + k))
    vd()
    ve.tensor_mul(sc_col("sqsk"), sc_col("s_q"), sc_col("s_k"))
    vd()
    sDV.inc(ve.engine_nop())
    bcast(bc_col("sqsk"), sc_col("sqsk"))
    barrier()

    with nc.sbuf_tensor("qt4", [P, 2 * S], F32) as qt4, \
         nc.sbuf_tensor("qi4", [P, 2 * S], I32) as qi4, \
         nc.sbuf_tensor("qb4", [P, 2 * S], BF16) as qb4:
        for k, tok, spill, qspill in (("q", TOWN, q_sp, nq_sp),
                                      ("k", S, k_sp, nk_sp),
                                      ("v", S, v_sp, nv_sp)):
            quant_stream(spill, 16, tok, bc_col("inv_" + k),
                         xbuf=qt4, ibuf=qi4, dst=qspill)
            barrier()
        barrier()
        for c in range(32):
            buf = qb4[:, (c % 2) * S:(c % 2) * S + DM]
            ld = LD.issue(sy, c, buf, nv_sp[:, c * P:(c + 1) * P],
                          transpose=True)
            wck(gp, ld)
            spk = SP.issue(gp, c,
                           blk_w(nvtm_bk, c, DM).rearrange("(c2 p) m -> p c2 m",
                                                           p=P),
                           buf.rearrange("p (c2 m) -> p c2 m", m=P))
            wck(sy, spk)
    barrier()

    NQT = TOWN // P
    misc_close()

    def make_ffnq_pump(fqx, fqi, fqb):
        jobs = []
        for (w_in, w_bk, nrows, name) in ((wu_t, tu_bk, DM, "u"),
                                          (wg_t, tg_bk, DM, "g"),
                                          (wd_t, td_bk, DE, "d")):
            fd = w_in.shape[1]
            nh_ = 2 if fd == DE else 1
            cfd = fd // nh_
            for i in range(nrows // P):
                for hf in range(nh_):
                    jobs.append((w_in, w_bk, i, hf, fd, cfd, name))

        def gen():
            ac_d = {}
            cv_d = {}
            sp_d = {}
            for j, (w_in, w_bk, i, hf, fd, cfd, name) in enumerate(jobs):
                buf = fqx[:, (j % 2) * 4096:(j % 2) * 4096 + cfd]
                if j >= 2:
                    wck(sy, ac_d[j - 2])
                ld = F1.issue(sy, j, buf,
                              w_in[i * P:(i + 1) * P,
                                   hf * cfd:(hf + 1) * cfd])
                wck(ac, ld)
                if j >= 2:
                    wck(ac, cv_d[j - 2])
                ib = fqi[:, (j % 2) * 4096:(j % 2) * 4096 + cfd]
                sAC.inc(ac.activation(ib, buf, AF.Identity,
                                      scale=bc_col("tinv" + name)))
                ac_d[j] = sAC.ck()
                ob = fqb[:, (j % 2) * 4096:(j % 2) * 4096 + cfd]
                wck(ve, ac_d[j])
                if j >= 2:
                    wck(ve, sp_d[j - 2])
                sDV.inc(ve.tensor_scalar(ob, ib, -1.0, 1.0,
                                         ALU.max, ALU.min))
                cv_d[j] = sDV.ck()
                wck(gp, cv_d[j])
                dst = w_bk[i * fd + hf * cfd:i * fd + (hf + 1) * cfd, :]
                sp_d[j] = F2.issue(
                    gp, j, dst.rearrange("(c p) m -> p c m", p=P),
                    ob.rearrange("p (c m) -> p c m", m=P))
                yield
        return gen()

    with nc.sbuf_tensor("fqx", [P, 2 * 4096], F32) as fqx, \
         nc.sbuf_tensor("fqi", [P, 2 * 4096], I32) as fqi, \
         nc.sbuf_tensor("fqb", [P, 2 * 4096], BF16) as fqb, \
         nc.sbuf_tensor("p5q", [P, 2 * TOWN], BF16) as p5q, \
         nc.sbuf_tensor("p5k", [P, 2 * S], BF16) as p5k, \
         nc.sbuf_tensor("p5e", [P, 2048], F32) as p5e, \
         nc.sbuf_tensor("p5t", [P, 16], F32) as p5t, \
         nc.sbuf_tensor("zh_arr", [P, 32], F32) as zh_arr, \
         nc.sbuf_tensor("ea_arr", [P, 32], F32) as ea_arr, \
         nc.sbuf_tensor("ee_arr", [P, 32], F32) as ee_arr, \
         nc.sbuf_tensor("tt_arr", [P, 32], F32) as tt_arr, \
         nc.sbuf_tensor("zrow", [P, 16], F32) as zrow, \
         nc.sbuf_tensor("zred", [P, 1], F32) as zred, \
         nc.psum_tensor("p1_ps", [P, S], F32) as p1_ps:
        pump = make_ffnq_pump(fqx, fqi, fqb)

        def pump_n(n):
            for _ in range(n):
                try:
                    next(pump)
                except StopIteration:
                    return
        exp_cks = {}
        head_used = [None] * NHEADS
        merge_ck = None
        ee_ck = None
        for h in range(NHEADS):
            qb = p5q[:, (h % 2) * TOWN:(h % 2) * TOWN + TOWN]
            kb = p5k[:, (h % 2) * S:(h % 2) * S + S]
            if h >= 2:
                wck(sy, head_used[h - 2])
            ld1 = LD.issue(sy, h, qb, nq_sp[h * P:(h + 1) * P, :])
            ld2 = LD2.issue(sy, h, kb, nk_sp[h * P:(h + 1) * P, :])
            for i in range(NQT):
                idx = h * NQT + i
                sl = 8 * (idx % 2)
                mh = p5t[:, sl + 0:sl + 2]
                nb = p5t[:, sl + 2:sl + 4]
                mi = p5t[:, sl + 4:sl + 5]
                te_ck = [None, None]
                for half in range(2):
                    wck(te, ld1)
                    wck(te, ld2)
                    if idx > 0:
                        wck(te, exp_cks[(idx - 1, half)])
                    for cchunk in range(4):
                        off = half * 2048 + cchunk * 512
                        ins = te.matmul(p1_ps[:, off:off + 512],
                                        qb[:, i * P:(i + 1) * P],
                                        kb[:, off:off + 512],
                                        start=True, stop=True)
                    sTE.inc(ins)
                    te_ck[half] = sTE.ck()
                wck(ve, te_ck[0])
                if idx > 1:
                    wck(ve, exp_cks[(idx - 2, 1)])
                if i == 0 and ee_ck is not None:
                    wck(ve, ee_ck)
                ve.tensor_reduce(mh[:, 0:1], p1_ps[:, 0:2048], AX, ALU.max)
                vd()
                ve.tensor_scalar(nb[:, 0:1], mh[:, 0:1], bc_col("sqsk"),
                                 -1.0, ALU.mult, ALU.mult)
                vd()
                sDV.inc(ve.engine_nop())
                dv0 = sDV.ck()
                wck(ac, dv0)
                if i == 0 and merge_ck is not None:
                    wck(ac, merge_ck)
                sAC.inc(ac.activation(p5e[:, :], p1_ps[:, 0:2048], AF.Exp,
                                      bias=nb[:, 0:1], scale=bc_col("sqsk"),
                                      accum_out=zh_arr[:, 2 * i:2 * i + 1]))
                exp_cks[(idx, 0)] = sAC.ck()
                wck(ve, te_ck[1])
                ve.tensor_reduce(mh[:, 1:2], p1_ps[:, 2048:4096], AX, ALU.max)
                vd()
                ve.tensor_scalar(nb[:, 1:2], mh[:, 1:2], bc_col("sqsk"),
                                 -1.0, ALU.mult, ALU.mult)
                ve.tensor_tensor(mi, mh[:, 0:1], mh[:, 1:2], ALU.max)
                vd()
                ve.tensor_copy(mstat[:, idx:idx + 1], mi)
                ve.tensor_scalar(ea_arr[:, 2 * i:2 * i + 2], mh, mi,
                                 bc_col("sqsk"), ALU.subtract, ALU.mult)
                vd()
                sDV.inc(ve.engine_nop())
                dv1 = sDV.ck()
                wck(ac, dv1)
                sAC.inc(ac.activation(p5e[:, :], p1_ps[:, 2048:4096], AF.Exp,
                                      bias=nb[:, 1:2], scale=bc_col("sqsk"),
                                      accum_out=zh_arr[:, 2 * i + 1:2 * i + 2]))
                exp_cks[(idx, 1)] = sAC.ck()
                pump_n(1)
            sDV.wait(ac)
            sAC.inc(ac.activation(ee_arr[:, :], ea_arr[:, :], AF.Exp))
            ee_ck = sAC.ck()
            sAC.wait(ve)
            ve.tensor_mul(tt_arr[:, :], zh_arr[:, :], ee_arr[:, :])
            vd()
            ve.tensor_add(zrow[:, :],
                          tt_arr.rearrange("p (i two) -> p i two", two=2)[:, :, 0],
                          tt_arr.rearrange("p (i two) -> p i two", two=2)[:, :, 1])
            vd()
            ve.tensor_copy(zstat[:, h * 16:(h + 1) * 16], zrow[:, :])
            ve.tensor_reduce(zred[:, :], zrow[:, :], AX, ALU.min)
            vd()
            sDV.inc(ve.tensor_tensor(zmin[:, :], zmin[:, :], zred[:, :],
                                     ALU.min))
            vd()
            sDV.inc(ve.engine_nop())
            merge_ck = sDV.ck()
            head_used[h] = sTE.ck()
    barrier()
    misc_open()

    preduce(sc_col("zmin"), zmin[:, :], ALU.min)
    collective(3, ["zmin"], ["zmin_g"], ALU.min)
    sDV.wait(ve)
    vd()
    ve.reciprocal(sc_col("amax_attn"), sc_col("zmin_g"))
    vd()
    fq_scale("amax_attn", "s_attn", "inv_attn")
    bcast(bc_col("s_attn"), sc_col("s_attn"))
    with nc.sbuf_tensor("btmp", [P, 256], F32) as btmp:
        sDV.wait(ve)
        ve.tensor_scalar(btmp[:, :], zstat[:, :], bc_col("s_attn"), None,
                         ALU.mult)
        sDV.inc(ve.engine_nop())
        sDV.wait(ac)
        sAC.inc(ac.activation(btmp[:, :], btmp[:, :], AF.Ln))
        sAC.wait(ve)
        ve.tensor_scalar(mstat[:, :], mstat[:, :], bc_col("sqsk"), None,
                         ALU.mult)
        vd()
        ve.tensor_add(btmp[:, :], btmp[:, :], mstat[:, :])
        vd()
        sDV.inc(ve.engine_nop())
        sDV.wait(te)
        with nc.sbuf_tensor("btTs", [P, 256], F32) as btTs:
            for j in range(2):
                sTE.inc(te.transpose(misc_ps[:, 0:P],
                                     btmp[:, j * P:(j + 1) * P], eye[:, :]))
                sTE.wait(ve)
                sDV.inc(ve.tensor_copy(btTs[:, j * P:(j + 1) * P],
                                       misc_ps[:, 0:P]))
                sDV.wait(te)
            sDV.wait(gp)
            sMS.dinc(gp.dma_start(
                out=bt_dram.rearrange("(j l) q -> l j q", l=P),
                in_=btTs.rearrange("l (j q) -> l j q", q=P)))
            sMS.wait(gp)
        sGP.inc(gp.engine_nop())
        sGP.wait(sy)
    vd()
    ve.tensor_mul(sc_col("dq_av"), sc_col("s_attn"), sc_col("s_v"))
    vd()
    sDV.inc(ve.engine_nop())
    bcast(bc_col("dq_av"), sc_col("dq_av"))
    barrier()
    if "mz" in dbg:
        sDV.wait(gp)
        sMS.dinc(gp.dma_start(out=dbg_t["mstat"][:, :], in_=mstat[:, :]))
        sMS.dinc(gp.dma_start(out=dbg_t["zstat"][:, :], in_=zstat[:, :]))
        sMS.dinc(gp.dma_start(out=dbg_t["brow"][:, :], in_=bt_dram[:, :]))
        sMS.wait(gp)
        barrier()

    misc_close()
    with nc.sbuf_tensor("brow", [1, 2 * TOWN], F32) as brow, \
         nc.sbuf_tensor("p6q", [P, 2 * TOWN], BF16) as p6q, \
         nc.sbuf_tensor("p6k", [P, 2 * S], BF16) as p6k, \
         nc.sbuf_tensor("p6v", [P, 2 * S], BF16) as p6v, \
         nc.sbuf_tensor("bbc", [P, 2048], F32) as bbc, \
         nc.sbuf_tensor("ein", [P, 4 * 2048], F32) as ein, \
         nc.sbuf_tensor("ei32", [P, 4 * 2048], I32) as ei32, \
         nc.sbuf_tensor("ptb", [P, 4 * 2048], BF16) as ptb, \
         nc.sbuf_tensor("ostg", [P, 2 * 2048], F32) as ostg, \
         nc.sbuf_tensor("ored", [P, 1], F32) as ored, \
         nc.psum_tensor("pl_ps", [P, 4 * 512], F32) as pl_ps, \
         nc.psum_tensor("av_ps", [P, 4 * 512], F32) as av_ps:
        head_used = [None] * NHEADS
        evac_done = [None] * NHEADS
        ost_done = [None] * NHEADS
        stt_all = {}
        exp_all = {}
        cp_all = {}
        av_all = {}
        for h in range(NHEADS):
            qb = p6q[:, (h % 2) * TOWN:(h % 2) * TOWN + TOWN]
            kb = p6k[:, (h % 2) * S:(h % 2) * S + S]
            vb = p6v[:, (h % 2) * S:(h % 2) * S + S]
            if h >= 2:
                wck(sy, head_used[h - 2])
            ld1 = LD.issue(sy, h, qb, nq_sp[h * P:(h + 1) * P, :])
            ld2 = LD2.issue(sy, h, kb, nk_sp[h * P:(h + 1) * P, :])
            ld3 = LD2.issue(
                sy, h, vb.rearrange("p (c m) -> p c m", m=P),
                blk_r(nvtm_bk, h, DM, S))
            browh = brow[0:1, (h % 2) * TOWN:(h % 2) * TOWN + TOWN]
            ldb = LD2.issue(
                sy, h, browh,
                bt_dram[h * 16:(h + 1) * 16, :].rearrange("a b -> (a b)")[None, :])
            for qc in range(4):
                sDV.wait(te)
                wck(te, ldb)
                sTE.inc(te.matmul(
                    pl_ps[:, 0:512], ones_row[:, :],
                    browh[0:1, qc * 512:(qc + 1) * 512],
                    start=True, stop=True))
                sTE.wait(ve)
                sDV.inc(ve.tensor_copy(bbc[:, qc * 512:(qc + 1) * 512],
                                       pl_ps[:, 0:512]))
                sDV.wait(te)
            bias_dv = sDV.ck()
            def emit_av(kt):
                gkt_ = h * 32 + kt
                pbv = ptb[:, (kt % 4) * 2048:(kt % 4) * 2048 + 2048]
                wck(te, cp_all[gkt_])
                wck(te, ld3)
                if kt == 0 and h >= 1:
                    wck(te, evac_done[h - 1])
                for qc in range(4):
                    ins = te.matmul(av_ps[:, qc * 512:(qc + 1) * 512],
                                    vb[:, kt * P:(kt + 1) * P],
                                    pbv[:, qc * 512:(qc + 1) * 512],
                                    start=(kt == 0), stop=(kt == 31))
                sTE.inc(ins)
                av_all[gkt_] = sTE.ck()

            for kt in range(32):
                gkt = h * 32 + kt
                pb = ptb[:, (kt % 4) * 2048:(kt % 4) * 2048 + 2048]
                ei = ei32[:, (kt % 4) * 2048:(kt % 4) * 2048 + 2048]
                einb = ein[:, (kt % 4) * 2048:(kt % 4) * 2048 + 2048]
                for qc in range(4):
                    git = gkt * 4 + qc
                    ps = pl_ps[:, qc * 512:qc * 512 + 512]
                    wck(te, ld1)
                    wck(te, ld2)
                    if kt == 0:
                        wck(te, bias_dv)
                    wck(te, stt_all.get(git - 4))
                    sTE.inc(te.matmul(ps, kb[:, kt * P:(kt + 1) * P],
                                      qb[:, qc * 512:(qc + 1) * 512],
                                      start=True, stop=True))
                    lck = sTE.ck()
                    wck(ve, lck)
                    if qc == 0:
                        wck(ve, exp_all.get(gkt - 4))
                    sDV.inc(ve.scalar_tensor_tensor(
                        einb[:, qc * 512:(qc + 1) * 512], ps, bc_col("sqsk"),
                        bbc[:, qc * 512:(qc + 1) * 512],
                        ALU.mult, ALU.subtract))
                    stt_all[git] = sDV.ck()
                wck(ac, stt_all[git])
                wck(ac, cp_all.get(gkt - 4))
                sAC.inc(ac.activation(ei, einb, AF.Exp))
                exp_all[gkt] = sAC.ck()
                wck(gp, exp_all[gkt])
                if gkt >= 4:
                    wck(gp, av_all.get(gkt - 4))
                cp_all[gkt] = CS.issue(gp, gkt, pb, ei)
                if kt >= 3:
                    emit_av(kt - 3)
            emit_av(29)
            emit_av(30)
            emit_av(31)
            head_used[h] = sTE.ck()
            ostg_h = ostg[:, (h % 2) * 2048:(h % 2) * 2048 + 2048]
            sTE.wait(ac)
            if h >= 2:
                for c_ in ost_done[h - 2]:
                    wck(ac, c_)
            for qc in range(4):
                sAC.inc(ac.activation(ostg_h[:, qc * 512:(qc + 1) * 512],
                                      av_ps[:, qc * 512:(qc + 1) * 512],
                                      AF.Identity, scale=bc_col("dq_av")))
            evac_done[h] = sAC.ck()
            sAC.wait(ve)
            ve.tensor_reduce(ored[:, :], ostg_h, AX, ALU.max,
                             apply_absolute_value=True)
            vd()
            sDV.inc(ve.tensor_tensor(am_col("out"), am_col("out"),
                                     ored[:, :], ALU.max))
            vd()
            dvh = sDV.ck()
            sAC.wait(gp)
            spk = SP.issue(gp, h, out_sp[h * P:(h + 1) * P, :], ostg_h)
            ost_done[h] = [spk, dvh]
        SP.wait_all(gp)
    barrier()
    misc_open()
    if "attnout" in dbg:
        dbg_dump_dram("attnout", out_sp, DM, TOWN, F32)
        barrier()

    preduce(sc_col("amax_out"), am_col("out"), ALU.max)
    collective(4, ["amax_out"], ["gout"], ALU.max)
    sDV.wait(ve)
    fq_scale("gout", "s_out", "inv_out")
    bcast(bc_col("inv_out"), sc_col("inv_out"))
    vd()
    ve.tensor_mul(sc_col("dq_p"), sc_col("s_out"), sc_col("s_wp"))
    vd()
    sDV.inc(ve.engine_nop())
    bcast(bc_col("dq_p"), sc_col("dq_p"))
    barrier()

    st_no = contextlib.ExitStack()
    n_out = st_no.enter_context(nc.sbuf_tensor("n_out", [P, 16 * TOWN], BF16))
    with nc.sbuf_tensor("p7t", [P, 2 * TOWN], F32) as p7t, \
         nc.sbuf_tensor("p7i", [P, TOWN], I32) as p7i:
        ac_done = [None] * 16
        cv_done = [None] * 16
        for i in range(16):
            buf = p7t[:, (i % 2) * TOWN:(i % 2) * TOWN + TOWN]
            if i >= 2:
                wck(sy, ac_done[i - 2])
            ld = LD.issue(sy, i, buf, out_sp[i * P:(i + 1) * P, :])
            wck(ac, ld)
            if i >= 1:
                wck(ac, cv_done[i - 1])
            sAC.inc(ac.activation(p7i[:, :], buf, AF.Identity,
                                  scale=bc_col("inv_out")))
            ac_done[i] = sAC.ck()
            sAC.wait(gp)
            sGP.inc(gp.tensor_copy(n_out[:, i * TOWN:(i + 1) * TOWN],
                                   p7i[:, :]))
            cv_done[i] = sGP.ck()
    barrier()

    with nc.sbuf_tensor("nwb7", [P, 2 * DM], BF16) as nwb7, \
         nc.sbuf_tensor("p7s", [P, 4 * 512], F32) as p7s, \
         nc.sbuf_tensor("x1s", [P, 4 * 512], F32) as x1s, \
         nc.sbuf_tensor("xfh", [P, 2 * 512], F32) as xfh, \
         nc.psum_tensor("proj7_ps", [P, 2 * 512], F32) as proj_ps:
        it = 0
        w_used = {}
        ac_done = {}
        dv_done = {}
        sp_done = {}
        for tc in range(4):
            for oc in range(16):
                if it >= 2:
                    wck(sy, w_used.get(it - 2))
                wbuf = nwb7[:, (it % 2) * DM:(it % 2) * DM + DM]
                ld = LD.issue(
                    sy, it,
                    wbuf.rearrange("p (j m) -> p j m", m=P),
                    blk_r(nw_bk["p"], oc, DM, DM))
                if it >= 2:
                    wck(sy, dv_done.get(it - 2))
                ldx = LD2.issue(sy, it,
                                xfh[:, (it % 2) * 512:(it % 2) * 512 + 512],
                                x_fmh[oc * P:(oc + 1) * P,
                                      tc * 512:(tc + 1) * 512])
                ps = proj_ps[:, (it % 2) * 512:(it % 2) * 512 + 512]
                wck(te, ld)
                if it >= 2:
                    wck(te, ac_done.get(it - 2))
                for j in range(16):
                    ins = te.matmul(ps, wbuf[:, j * P:(j + 1) * P],
                                    n_out[:, j * TOWN + tc * 512:
                                          j * TOWN + tc * 512 + 512],
                                    start=(j == 0), stop=(j == 15))
                sTE.inc(ins)
                stg = p7s[:, (it % 4) * 512:(it % 4) * 512 + 512]
                x1g = x1s[:, (it % 4) * 512:(it % 4) * 512 + 512]
                sTE.wait(ac)
                if it >= 4:
                    wck(ac, sp_done.get(it - 4))
                sAC.inc(ac.activation(stg, ps, AF.Identity,
                                      bias=bias_sb["p"][:, oc:oc + 1],
                                      scale=bc_col("dq_p")))
                ac_done[it] = sAC.ck()
                sAC.wait(ve)
                wck(ve, ldx)
                sDV.inc(ve.tensor_add(
                    x1g, stg, xfh[:, (it % 2) * 512:(it % 2) * 512 + 512]))
                dv_done[it] = sDV.ck()
                sDV.wait(gp)
                sp_done[it] = SP.issue(
                    gp, it,
                    x1_sp[oc * P:(oc + 1) * P, tc * 512:(tc + 1) * 512], x1g)
                w_used[it] = sTE.ck()
                it += 1
        SP.wait_all(gp)
    st_no.close()
    barrier()
    misc_close()
    with nc.sbuf_tensor("x1b", [P, 2 * DM], F32) as x1b, \
         nc.sbuf_tensor("sq7", [P, 2 * DM], F32) as sq7, \
         nc.psum_tensor("st1_ps", [P, 4 * 512], F32) as st1_ps, \
         nc.psum_tensor("st2_ps", [P, 4 * 512], F32) as st2_ps:
        ac7 = {}
        te7 = {}
        for i in range(16):
            buf = x1b[:, (i % 2) * DM:(i % 2) * DM + DM]
            sqb = sq7[:, (i % 2) * DM:(i % 2) * DM + DM]
            if i >= 2:
                wck(sy, te7[i - 2])
            ld = LD.issue(sy, i, buf, x1_sp[i * P:(i + 1) * P, :])
            wck(ac, ld)
            if i >= 2:
                wck(ac, te7[i - 2])
            sAC.inc(ac.activation(sqb, buf, AF.Square))
            ac7[i] = sAC.ck()
            wck(te, ld)
            for tc in range(4):
                te.matmul(st1_ps[0:1, tc * 512:(tc + 1) * 512], ones_col[:, :],
                          buf[:, tc * 512:(tc + 1) * 512],
                          start=(i == 0), stop=(i == 15),
                          skip_group_check=True)
            wck(te, ac7[i])
            for tc in range(4):
                ins = te.matmul(st2_ps[0:1, tc * 512:(tc + 1) * 512],
                                ones_col[:, :],
                                sqb[:, tc * 512:(tc + 1) * 512],
                                start=(i == 0), stop=(i == 15),
                                skip_group_check=True)
            sTE.inc(ins)
            te7[i] = sTE.ck()
        sTE.wait(ve)
        ve.tensor_scalar(srow[0:1, :], st1_ps[0:1, 0:2048], 1.0 / DM,
                         None, ALU.mult)
        ve.tensor_scalar(mrrow[0:1, :], st2_ps[0:1, 0:2048], 1.0 / DM,
                         LN_EPS, ALU.mult, ALU.add)
        vd()
        sDV.inc(ve.engine_nop())
        sDV.wait(te)
    misc_open()
    with nc.sbuf_tensor("m2row", [1, 2048], F32) as m2row:
        ve.tensor_mul(m2row[0:1, :], srow[0:1, :], srow[0:1, :])
        vd()
        ve.tensor_sub(mrrow[0:1, :], mrrow[0:1, :], m2row[0:1, :])
        vd()
        sDV.inc(ve.engine_nop())
        sDV.wait(ac)
        sAC.inc(ac.activation(m2row[0:1, :], mrrow[0:1, :], AF.Sqrt))
        sAC.wait(ve)
        ve.reciprocal(m2row[0:1, :], m2row[0:1, :])
        vd()
        ve.tensor_mul(mrrow[0:1, :], srow[0:1, :], m2row[0:1, :])
        ve.tensor_copy(srow[0:1, :], m2row[0:1, :])
        vd()
        sDV.inc(ve.engine_nop())
    barrier()
    if "x1" in dbg:
        dbg_dump_dram("x1", x1_sp, DM, TOWN, F32)
        barrier()

    with nc.sbuf_tensor("x1l", [P, 16 * 512], BF16) as x1l, \
         nc.sbuf_tensor("hid", [P, 64 * 512], BF16) as hid, \
         nc.sbuf_tensor("rbc", [P, 512], F32) as rbc, \
         nc.sbuf_tensor("mrbc", [P, 512], F32) as mrbc, \
         nc.sbuf_tensor("x1c", [P, 2 * 512], F32) as x1c, \
         nc.sbuf_tensor("ugw", [P, 2 * 2 * DM], BF16) as ugw, \
         nc.sbuf_tensor("dww", [P, 2 * DE], BF16) as dww, \
         nc.sbuf_tensor("silu", [P, 2 * 512], F32) as silu_t, \
         nc.sbuf_tensor("fstg", [P, 2 * 512], F32) as fstg, \
         nc.psum_tensor("ffn_ps", [P, 6 * 512], F32) as ffn_ps:
        for name in ("u", "g", "d"):
            bcast(bc_col("tscale" + name), sc_col("tscale" + name))
        s1_used = None
        s2_hid_used = None
        stt_hid_last = None
        stt_out_last2 = [None, None]
        p9_prev_dv = None
        for tc in range(4):
            sDV.wait(te)
            sTE.inc(te.matmul(misc_ps[:, 0:512], ones_row[:, :],
                              srow[0:1, tc * 512:(tc + 1) * 512],
                              start=True, stop=True))
            sTE.wait(ve)
            if s1_used is not None:
                wck(ve, s1_used)
            sDV.inc(ve.tensor_copy(rbc[:, :], misc_ps[:, 0:512]))
            sDV.wait(te)
            sTE.inc(te.matmul(misc_ps[:, 0:512], ones_row[:, :],
                              mrrow[0:1, tc * 512:(tc + 1) * 512],
                              start=True, stop=True))
            sTE.wait(ve)
            sDV.inc(ve.tensor_copy(mrbc[:, :], misc_ps[:, 0:512]))
            sDV.wait(te)
            dv_done = [None] * 16
            for oc in range(16):
                cbuf = x1c[:, (oc % 2) * 512:(oc % 2) * 512 + 512]
                if oc >= 2:
                    wck(sy, dv_done[oc - 2])
                elif p9_prev_dv is not None:
                    wck(sy, p9_prev_dv)
                ld = LD.issue(sy, oc, cbuf,
                              x1_sp[oc * P:(oc + 1) * P,
                                    tc * 512:(tc + 1) * 512])
                wck(ve, ld)
                ve.tensor_mul(cbuf, cbuf, rbc[:, :])
                sDV.inc(ve.tensor_sub(x1l[:, oc * 512:(oc + 1) * 512],
                                      cbuf, mrbc[:, :]))
                dv_done[oc] = sDV.ck()
            x1l_ready = sDV.ck()
            ug_used = dict(ug_prev) if tc > 0 else {}
            stt_hid = {}
            for ec in range(64):
                ub = ugw[:, (ec % 2) * 2 * DM:(ec % 2) * 2 * DM + DM]
                gb2 = ugw[:, (ec % 2) * 2 * DM + DM:(ec % 2) * 2 * DM + 2 * DM]
                if ec >= 2:
                    wck(sy, ug_used[ec - 2])
                elif tc > 0:
                    wck(sy, ug_used[62 + ec])
                lu = LD.issue(
                    sy, ec, ub.rearrange("p (j m) -> p j m", m=P),
                    blk_r(tu_bk, ec, DE, DM))
                lg = LD2.issue(
                    sy, ec, gb2.rearrange("p (j m) -> p j m", m=P),
                    blk_r(tg_bk, ec, DE, DM))
                s1b = (ec % 2) * 1024
                wck(te, lu)
                wck(te, lg)
                if ec == 0:
                    for j_ in range(16):
                        wck(te, dv_done[j_])
                else:
                    wck(te, x1l_ready)
                if ec >= 2:
                    wck(te, stt_hid[ec - 2])
                elif stt_hid_last is not None:
                    wck(te, stt_hid_last)
                if ec <= 1 and s2_hid_used is not None:
                    wck(te, s2_hid_used)
                for j in range(16):
                    te.matmul(ffn_ps[:, s1b:s1b + 512],
                              ub[:, j * P:(j + 1) * P],
                              x1l[:, j * 512:(j + 1) * 512],
                              start=(j == 0), stop=(j == 15))
                for j in range(16):
                    i2 = te.matmul(ffn_ps[:, s1b + 512:s1b + 1024],
                                   gb2[:, j * P:(j + 1) * P],
                                   x1l[:, j * 512:(j + 1) * 512],
                                   start=(j == 0), stop=(j == 15))
                sTE.inc(i2)
                ug_used[ec] = sTE.ck()
                wck(ac, ug_used[ec])
                if ec >= 2:
                    wck(ac, stt_hid[ec - 2])
                elif stt_hid_last is not None:
                    wck(ac, stt_hid_last)
                sAC.inc(ac.activation(silu_t[:, (ec % 2) * 512:
                                              (ec % 2) * 512 + 512],
                                      ffn_ps[:, s1b:s1b + 512], AF.Silu,
                                      scale=bc_col("tscaleu")))
                sil_ck = sAC.ck()
                wck(ve, sil_ck)
                sDV.inc(ve.scalar_tensor_tensor(
                    hid[:, ec * 512:(ec + 1) * 512],
                    ffn_ps[:, s1b + 512:s1b + 1024],
                    bc_col("tscaleg"),
                    silu_t[:, (ec % 2) * 512:(ec % 2) * 512 + 512],
                    ALU.mult, ALU.mult))
                stt_hid[ec] = sDV.ck()
            s1_used = sTE.ck()
            ug_prev = {62: ug_used[62], 63: ug_used[63]}
            stt_hid_last = stt_hid[63]
            hid_ready = sDV.ck()
            dw_used = dict(dw_prev) if tc > 0 else {}
            stt_out = {}
            sp_out = {}
            for oc in range(16):
                db = dww[:, (oc % 2) * DE:(oc % 2) * DE + DE]
                if oc >= 2:
                    wck(sy, dw_used[oc - 2])
                elif tc > 0:
                    wck(sy, dw_used[14 + oc])
                ldw = LD.issue(
                    sy, oc, db.rearrange("p (j m) -> p j m", m=P),
                    blk_r(td_bk, oc, DM, DE))
                if oc >= 2:
                    wck(sy, stt_out[oc - 2])
                elif x1l_ready is not None:
                    wck(sy, x1l_ready)
                ldx = LD2.issue(sy, oc,
                                x1c[:, (oc % 2) * 512:(oc % 2) * 512 + 512],
                                x1_sp[oc * P:(oc + 1) * P,
                                      tc * 512:(tc + 1) * 512])
                ps = ffn_ps[:, (4 + oc % 2) * 512:(4 + oc % 2) * 512 + 512]
                wck(te, ldw)
                wck(te, hid_ready)
                if oc >= 2:
                    wck(te, stt_out[oc - 2])
                elif stt_out_last2[oc] is not None:
                    wck(te, stt_out_last2[oc])
                for j in range(64):
                    ins = te.matmul(ps, db[:, j * P:(j + 1) * P],
                                    hid[:, j * 512:(j + 1) * 512],
                                    start=(j == 0), stop=(j == 63))
                sTE.inc(ins)
                dw_used[oc] = sTE.ck()
                stg = fstg[:, (oc % 2) * 512:(oc % 2) * 512 + 512]
                sTE.wait(ve)
                wck(ve, ldx)
                if oc >= 2:
                    wck(ve, sp_out[oc - 2])
                sDV.inc(ve.scalar_tensor_tensor(
                    stg, ps, bc_col("tscaled"),
                    x1c[:, (oc % 2) * 512:(oc % 2) * 512 + 512],
                    ALU.mult, ALU.add))
                stt_out[oc] = sDV.ck()
                sDV.wait(gp)
                sp_out[oc] = SP.issue(
                    gp, oc,
                    out_fm[oc * P:(oc + 1) * P, tc * 512:(tc + 1) * 512], stg)
            s2_hid_used = sTE.ck()
            dw_prev = {14: dw_used[14], 15: dw_used[15]}
            stt_out_last2 = [stt_out[14], stt_out[15]]
            p9_prev_dv = sDV.ck()
            SP.wait_all(gp)
        sGP.inc(gp.engine_nop())
        sGP.wait(sy)
        sGP.wait(te)
    if "nwq" in dbg:
        dbg_dump_dram("nwq", nw_bk["q"], DM * 16, P, BF16)
        barrier()
    if "sc" in dbg:
        sMS.dinc(gp.dma_start(out=dbg_t["sc"][0:1, :], in_=sc[0:1, :]))
        sMS.wait(gp)
        barrier()
    st.close()
    return nc


DBG_SHAPES = {
    "sc": ([1, 96], F32),
    "nwq": ([DM * 16, P], BF16),
    "mstat": ([P, 256], F32),
    "zstat": ([P, 256], F32),
    "brow": ([256, P], F32),
    "nh": ([S, DM], BF16),
    "q": ([DM, TOWN], F32),
    "k": ([DM, S], F32),
    "v": ([DM, S], F32),
    "attnout": ([DM, TOWN], F32),
    "x1": ([DM, TOWN], F32),
}


def make_in_maps(inputs):
    x = np.ascontiguousarray(inputs["L0_attn_hidden_states"], np.float32)
    shared = {}
    for k, key in (("q", "L0_attn_Wq"), ("k", "L0_attn_Wk"),
                   ("v", "L0_attn_Wv"), ("p", "L0_attn_Wp")):
        shared[f"w{k}_t"] = np.ascontiguousarray(
            np.asarray(inputs[key], np.float32).T)
    for k, key in (("q", "L0_attn_bq"), ("k", "L0_attn_bk"),
                   ("v", "L0_attn_bv"), ("p", "L0_attn_bp")):
        shared[f"b{k}2d"] = np.ascontiguousarray(
            np.asarray(inputs[key], np.float32).reshape(16, P).T)
    shared["wu_t"] = np.ascontiguousarray(
        np.asarray(inputs["L0_ffn_W_up"], np.float32).T)
    shared["wg_t"] = np.ascontiguousarray(
        np.asarray(inputs["L0_ffn_W_gate"], np.float32).T)
    shared["wd_t"] = np.ascontiguousarray(
        np.asarray(inputs["L0_ffn_W_down"], np.float32).T)
    shared["eye128"] = np.eye(P, dtype=np.float32)
    maps = []
    for c in range(N_CORES):
        b, half = c // 2, c % 2
        xb = np.asarray(x[b], np.float32)
        x_tok_ = np.roll(xb, -half * TOWN, axis=0)
        m = {"x_tok": np.ascontiguousarray(x_tok_),
             "x_fmh": np.ascontiguousarray(x_tok_[:TOWN].T)}
        m.update(shared)
        maps.append(m)
    return maps


def kernel(**inputs):
    nc = build_program()
    maps = make_in_maps(inputs)
    res = run_bass_kernel_spmd(nc, maps, core_ids=list(range(N_CORES)))
    out = np.empty((4, S, DM), np.float32)
    for c in range(N_CORES):
        b, half = c // 2, c % 2
        shard = np.asarray(res.results[c]["out_fm"])
        out[b, half * TOWN:(half + 1) * TOWN, :] = shard.T
    return out
